# version 02
# baseline (speedup 1.0000x reference)
import numpy as np
from contextlib import ExitStack

import concourse.bass as bass
import concourse.mybir as mybir
from concourse.bass_utils import run_bass_kernel_spmd

F32 = mybir.dt.float32
BF16 = mybir.dt.bfloat16
AF = mybir.ActivationFunctionType
ALU = mybir.AluOpType
AX = mybir.AxisListType

D = 1024
S = 4096
T = 2048
NP = 8
EPS = 1e-5
KT = S // 128
QC = T // 512
CW = 256


class Ctx:
    def __init__(self, nc, stack):
        self.nc = nc
        self.stack = stack
        self.sems = {}
        self.hwm = {}

    def sem(self, name, step=1):
        h = self.stack.enter_context(self.nc.semaphore(name))
        self.sems[name] = [h, step, 0]

    def inc(self, name, inst):
        e = self.sems[name]
        inst.then_inc(e[0], e[1])
        e[2] += e[1]
        return e[2]

    def val(self, name):
        return self.sems[name][2]

    def wait(self, ekey, eng, name, v):
        if v <= 0:
            return
        key = (ekey, name)
        if self.hwm.get(key, 0) >= v:
            return
        eng.wait_ge(self.sems[name][0], v)
        self.hwm[key] = v

    def barrier(self, engines):
        for ekey, eng in engines:
            for name in self.sems:
                self.wait(ekey, eng, name, self.val(name))


class Rel:
    def __init__(self, ctx):
        self.ctx = ctx
        self.rel = []

    def acquire(self, ekey, eng):
        for (sn, vv) in self.rel:
            self.ctx.wait(ekey, eng, sn, vv)
        self.rel = []

    def release(self, sem_name, val):
        self.rel.append((sem_name, val))


def build(taps=()):
    nc = bass.Bass("TRN2", target_bir_lowering=False)
    dr = lambda n, sh, kind: nc.dram_tensor(n, sh, F32, kind=kind).ap()

    x_full = dr("x_full", [D, S], "ExternalInput")
    x_own = dr("x_own", [D, T], "ExternalInput")
    wq = dr("wq", [128, 8 * 1024], "ExternalInput")
    wk = dr("wk", [128, 8 * 64], "ExternalInput")
    wv = dr("wv", [128, 8 * 64], "ExternalInput")
    wr = dr("wr", [128, 8 * 7], "ExternalInput")
    gate_t = dr("gate_t", [8, 4, 8, 128, 1024], "ExternalInput")
    up_t = dr("up_t", [8, 4, 8, 128, 1024], "ExternalInput")
    down_t = dr("down_t", [8, 4, 8, 128, 1024], "ExternalInput")
    out = dr("out", [D, T], "ExternalOutput")

    tap_aps = {}

    with ExitStack() as stack:
        ctx = Ctx(nc, stack)
        SB = lambda n, sh, dt=F32: stack.enter_context(nc.sbuf_tensor(n, sh, dt)).ap()
        sync, v, s, t, g = nc.sync, nc.vector, nc.scalar, nc.tensor, nc.gpsimd
        Ts, Ss, Vs, Gs, Qs = ("t", t), ("s", s), ("v", v), ("g", g), ("q", sync)
        ALL_ENG = [Ts, Ss, Vs, Gs, Qs]

        for name, step in ([("qg", 16), ("qxr", 16), ("pe", 1), ("act", 1),
                            ("dve", 1), ("pool", 1), ("st", 16)]
                           + [(f"qstg{i}", 16) for i in range(4)]
                           + [(f"qxf{i}", 16) for i in range(2)]):
            ctx.sem(name, step)
        W = lambda ee, name, val: ctx.wait(ee[0], ee[1], name, val)

        def tap(name, shape, src_ap, dep_sem):
            if name not in taps:
                return
            ap = dr("tap_" + name, shape, "ExternalOutput")
            tap_aps[name] = ap
            W(Qs, dep_sem, ctx.val(dep_sem))
            ctx.inc("st", sync.dma_start(ap, src_ap))

        XR = SB("XR", [128, 8 * T])
        PS = stack.enter_context(nc.psum_tensor("PS", [128, 4096], F32)).ap()
        bank = lambda i: PS[:, i * 512:(i + 1) * 512]
        brel = [Rel(ctx) for _ in range(8)]

        wk_b = SB("wk_b", [128, 8 * 64], BF16)
        wv_b = SB("wv_b", [128, 8 * 64], BF16)
        wr_b = SB("wr_b", [128, 8 * 7], BF16)
        ones_bf = SB("ones_bf", [128, 2], BF16)
        ones_f = SB("ones_f", [1, 128])
        eps_t = SB("eps_t", [1, 2])
        ident = SB("ident", [128, 128])
        srow = SB("srow", [1, 6 * 512])
        mrow = SB("mrow", [1, 16])
        arow = SB("arow", [1, 16])

        stg = [SB(f"stg{i}", [128, 8 * 128]) for i in range(4)]
        stg_rel = [Rel(ctx) for _ in range(4)]
        stg_i = [0]

        def stage_cast(src_ap, dst_bf_ap, n_cols):
            i = stg_i[0] % 4
            stg_i[0] += 1
            stg_rel[i].acquire(*Qs)
            dv = ctx.inc(f"qstg{i}", sync.dma_start(stg[i][:, 0:n_cols], src_ap))
            W(Gs, f"qstg{i}", dv)
            pv = ctx.inc("pool", g.tensor_copy(dst_bf_ap, stg[i][:, 0:n_cols]))
            stg_rel[i].release("pool", pv)
            return pv

        ctx.inc("dve", v.memset(ones_bf, 1.0))
        ctx.inc("dve", v.memset(ones_f, 1.0))
        consts_dve = ctx.inc("dve", v.memset(eps_t, EPS))
        ctx.inc("pool", g.memset(ident, 1.0))
        W(Gs, "pool", ctx.val("pool"))
        ident_done = ctx.inc("pool", g.affine_select(
            ident, ident, [[1, 128]], ALU.is_equal, 0.0, base=0, channel_multiplier=-1))

        xr_load = ctx.inc("qxr", sync.dma_start(
            XR.rearrange("p (c n) -> p c n", c=8),
            x_own.rearrange("(c p) n -> p c n", p=128)))

        wk_done = stage_cast(wk, wk_b, 8 * 64)
        wv_done = stage_cast(wv, wv_b, 8 * 64)
        wr_done = stage_cast(wr, wr_b, 8 * 7)

        def ln_stats(xbf_ap, xsq_ap, ready_pool, ready_act, cw):
            brel[0].acquire(*Ts)
            W(Ts, "dve", consts_dve)
            W(Ts, "pool", ready_pool)
            W(Ts, "act", ready_act)
            ins = None
            for c8 in range(8):
                ins = t.matmul(bank(0)[0:1, 0:cw], ones_bf[:, 0:1],
                               xbf_ap[:, c8 * cw:(c8 + 1) * cw],
                               start=(c8 == 0), stop=(c8 == 7))
            mv_sum = ctx.inc("pe", ins)
            brel[1].acquire(*Ts)
            for c8 in range(8):
                ins = t.matmul(bank(1)[0:1, 0:cw], ones_bf[:, 0:1],
                               xsq_ap[:, c8 * cw:(c8 + 1) * cw],
                               start=(c8 == 0), stop=(c8 == 7))
            mv_sq = ctx.inc("pe", ins)
            W(Vs, "pe", mv_sq)
            dv1 = ctx.inc("dve", v.tensor_scalar_mul(srow[0:1, 0:cw], bank(0)[0:1, 0:cw],
                                                     1.0 / D))
            dv2 = ctx.inc("dve", v.tensor_scalar_mul(srow[0:1, 512:512 + cw],
                                                     bank(1)[0:1, 0:cw], 1.0 / D))
            brel[0].release("dve", dv1)
            W(Vs, "dve", dv2)
            ctx.inc("dve", v.tensor_mul(srow[0:1, 1024:1024 + cw], srow[0:1, 0:cw],
                                        srow[0:1, 0:cw]))
            W(Vs, "dve", ctx.val("dve"))
            dvv = ctx.inc("dve", v.tensor_sub(srow[0:1, 1024:1024 + cw],
                                              srow[0:1, 512:512 + cw],
                                              srow[0:1, 1024:1024 + cw]))
            brel[1].release("dve", dvv)
            W(Ss, "dve", dvv)
            W(Ss, "dve", consts_dve)
            ctx.inc("act", s.activation(srow[0:1, 1536:1536 + cw], srow[0:1, 1024:1024 + cw],
                                        AF.Ln, bias=eps_t[0:1, 0:1], scale=1.0))
            W(Ss, "act", ctx.val("act"))
            sv2 = ctx.inc("act", s.activation(srow[0:1, 1536:1536 + cw],
                                              srow[0:1, 1536:1536 + cw], AF.Exp, scale=-0.5))
            W(Vs, "act", sv2)
            dvn = ctx.inc("dve", v.scalar_tensor_tensor(
                srow[0:1, 2048:2048 + cw], srow[0:1, 0:cw], -1.0, srow[0:1, 1536:1536 + cw],
                ALU.mult, ALU.mult))
            brel[1].acquire(*Ts)
            brel[2].acquire(*Ts)
            W(Ts, "act", sv2)
            W(Ts, "dve", dvn)
            t.matmul(bank(1)[:, 0:cw], ones_f, srow[0:1, 1536:1536 + cw],
                     start=True, stop=True)
            ins = t.matmul(bank(2)[:, 0:cw], ones_f, srow[0:1, 2048:2048 + cw],
                           start=True, stop=True)
            return ctx.inc("pe", ins)

        with ExitStack() as fa:
            SBA = lambda n, sh, dt=F32: fa.enter_context(nc.sbuf_tensor(n, sh, dt)).ap()
            Q = SBA("Q", [128, 8 * T], BF16)
            K2 = SBA("K2", [128, S], BF16)
            VA = SBA("VA", [128, KT * 65], BF16)
            va_ones = ctx.inc("dve", v.memset(VA, 1.0))

            with ExitStack() as fa1:
                SB1 = lambda n, sh, dt=F32: fa1.enter_context(nc.sbuf_tensor(n, sh, dt)).ap()
                wq_b = SB1("wq_b", [128, 8 * 1024], BF16)
                xf = [SB1(f"xf{i}", [128, 8 * CW]) for i in range(2)]
                xf_rel = [Rel(ctx) for _ in range(2)]
                xbf = [SB1(f"xbf{i}", [128, 8 * CW], BF16) for i in range(2)]
                xbf_rel = [Rel(ctx) for _ in range(2)]
                xsq = [SB1(f"xsq{i}", [128, 8 * CW], BF16) for i in range(2)]
                xsq_rel = [Rel(ctx) for _ in range(2)]
                lnb = [SB1(f"lnb{i}", [128, 8 * CW], BF16) for i in range(2)]
                lnb_rel = [Rel(ctx) for _ in range(2)]
                nrm = [SB1(f"nrm{i}", [128, CW]) for i in range(2)]
                nrm_rel = [Rel(ctx) for _ in range(2)]

                wq_done = 0
                for c8 in range(8):
                    wq_done = stage_cast(wq[:, c8 * 1024:(c8 + 1) * 1024],
                                         wq_b[:, c8 * 1024:(c8 + 1) * 1024], 1024)

                def ln_chunk(src_dram, bi, ln_out):
                    xf_rel[bi].acquire(*Qs)
                    dv = ctx.inc(f"qxf{bi}", sync.dma_start(
                        xf[bi].rearrange("p (c n) -> p c n", c=8), src_dram))
                    W(Gs, f"qxf{bi}", dv)
                    xbf_rel[bi].acquire(*Gs)
                    cv = ctx.inc("pool", g.tensor_copy(xbf[bi], xf[bi]))
                    W(Ss, f"qxf{bi}", dv)
                    xsq_rel[bi].acquire(*Ss)
                    sv = ctx.inc("act", s.activation(xsq[bi], xf[bi], AF.Square))
                    mvb = ln_stats(xbf[bi], xsq[bi], cv, sv, CW)
                    xbf_rel[bi].release("pe", mvb)
                    xsq_rel[bi].release("pe", mvb)
                    W(Vs, "pe", mvb)
                    lv = 0
                    for c8 in range(8):
                        ni = c8 % 2
                        nrm_rel[ni].acquire(*Vs)
                        dvt = ctx.inc("dve", v.tensor_tensor(
                            nrm[ni], xf[bi][:, c8 * CW:(c8 + 1) * CW],
                            bank(1)[:, 0:CW], ALU.mult))
                        W(Vs, "dve", dvt)
                        lv = ctx.inc("dve", v.tensor_tensor(
                            ln_out[:, c8 * CW:(c8 + 1) * CW], nrm[ni],
                            bank(2)[:, 0:CW], ALU.add))
                        nrm_rel[ni].release("dve", lv)
                    xf_rel[bi].release("dve", lv)
                    brel[1].release("dve", lv)
                    brel[2].release("dve", lv)
                    return lv

                xr_src = x_full.rearrange("(c p) n -> p c n", p=128)
                for j in range(S // CW):
                    bi = j % 2
                    lnb_rel[bi].acquire(*Vs)
                    lv = ln_chunk(xr_src[:, :, j * CW:(j + 1) * CW], bi, lnb[bi])
                    brel[3].acquire(*Ts)
                    W(Ts, "pool", wk_done)
                    W(Ts, "dve", lv)
                    ins = None
                    for c8 in range(8):
                        ins = t.matmul(bank(3)[0:64, 0:CW], wk_b[:, c8 * 64:(c8 + 1) * 64],
                                       lnb[bi][:, c8 * CW:(c8 + 1) * CW],
                                       start=(c8 == 0), stop=(c8 == 7))
                    mk = ctx.inc("pe", ins)
                    W(Ss, "pe", mk)
                    s.activation(K2[0:64, j * CW:(j + 1) * CW], bank(3)[0:64, 0:CW], AF.Copy)
                    se = ctx.inc("act", s.activation(K2[64:128, j * CW:(j + 1) * CW],
                                                     bank(3)[0:64, 0:CW], AF.Copy))
                    brel[3].release("act", se)
                    brel[4].acquire(*Ts)
                    W(Ts, "pool", wv_done)
                    ins = None
                    for kt4 in range(CW // 128):
                        for c8 in range(8):
                            ins = t.matmul(
                                PS[:, 4 * 512 + kt4 * 64: 4 * 512 + (kt4 + 1) * 64],
                                lnb[bi][:, c8 * CW + kt4 * 128: c8 * CW + (kt4 + 1) * 128],
                                wv_b[:, c8 * 64:(c8 + 1) * 64],
                                start=(c8 == 0), stop=(c8 == 7))
                    mvv = ctx.inc("pe", ins)
                    W(Vs, "pe", mvv)
                    W(Vs, "dve", va_ones)
                    kt0 = j * (CW // 128)
                    dvv2 = ctx.inc("dve", v.tensor_copy(
                        VA[:, kt0 * 65:(kt0 + CW // 128) * 65]
                        .rearrange("p (k n) -> p k n", k=CW // 128)[:, :, 0:64],
                        PS[:, 4 * 512: 4 * 512 + (CW // 128) * 64]
                        .rearrange("p (k n) -> p k n", k=CW // 128)))
                    brel[4].release("dve", dvv2)
                    lnb_rel[bi].release("pe", mvv)

                xo_src = x_own.rearrange("(c p) n -> p c n", p=128)
                for j in range(T // CW):
                    bi = j % 2
                    lnb_rel[bi].acquire(*Vs)
                    lv = ln_chunk(xo_src[:, :, j * CW:(j + 1) * CW], bi, lnb[bi])
                    for hp in range(8):
                        bk = 5 + (hp % 2)
                        brel[bk].acquire(*Ts)
                        W(Ts, "pool", wq_done)
                        W(Ts, "dve", lv)
                        ins = None
                        for c8 in range(8):
                            ins = t.matmul(
                                bank(bk)[:, 0:CW],
                                wq_b[:, c8 * 1024 + hp * 128: c8 * 1024 + (hp + 1) * 128],
                                lnb[bi][:, c8 * CW:(c8 + 1) * CW],
                                start=(c8 == 0), stop=(c8 == 7))
                        mq = ctx.inc("pe", ins)
                        W(Ss, "pe", mq)
                        qe = ctx.inc("act", s.activation(
                            Q[:, hp * T + j * CW: hp * T + (j + 1) * CW],
                            bank(bk)[:, 0:CW], AF.Copy))
                        brel[bk].release("act", qe)
                    lnb_rel[bi].release("pe", ctx.val("pe"))

                tap("lnq", [128, 8 * CW // 2], lnb[(T // CW - 1) % 2].bitcast(F32), "dve")
                ctx.barrier(ALL_ENG)

            p1_act = ctx.val("act")
            p1_dve = ctx.val("dve")
            tap("K2", [128, S // 2], K2.bitcast(F32), "act")
            tap("VA", [128, (KT * 65 + 1) // 2], VA.bitcast(F32), "dve")
            tap("Q", [128, 4 * T], Q.bitcast(F32), "act")

            with ExitStack() as fa2:
                SB2 = lambda n, sh, dt=F32: fa2.enter_context(nc.sbuf_tensor(n, sh, dt)).ap()
                ksq = SB2("ksq", [64, S], BF16)
                qsq = SB2("qsq", [128, T], BF16)
                qsq_rel = Rel(ctx)
                PT = [SB2(f"PT{i}", [128, 512], BF16) for i in range(4)]
                PT_rel = [Rel(ctx) for _ in range(4)]
                att_s = SB2("att_s", [128, 2 * 512])
                bias_hd = SB2("bias_hd", [128, 2])
                rdrow = SB2("rdrow", [1, 2 * 512])

                W(Ss, "act", p1_act)
                kq = ctx.inc("act", s.activation(ksq, K2[0:64, :], AF.Square))
                for c in range(8):
                    brel[7].acquire(*Ts)
                    W(Ts, "act", kq)
                    ins = t.matmul(bank(7)[0:1, :], ones_bf[0:64, 0:1],
                                   ksq[:, c * 512:(c + 1) * 512], start=True, stop=True)
                    mv = ctx.inc("pe", ins)
                    W(Vs, "pe", mv)
                    dv = ctx.inc("dve", v.reduce_max(mrow[0:1, c:c + 1], bank(7)[0:1, :],
                                                     axis=AX.X))
                    brel[7].release("dve", dv)
                W(Vs, "dve", ctx.val("dve"))
                ctx.inc("dve", v.reduce_max(mrow[0:1, 8:9], mrow[0:1, 0:8], axis=AX.X))

                for hp in range(NP):
                    qsq_rel.acquire(*Ss)
                    W(Ss, "act", p1_act)
                    sv = ctx.inc("act", s.activation(qsq, Q[:, hp * T:(hp + 1) * T],
                                                     AF.Square))
                    for hh in range(2):
                        rows = slice(64 * hh, 64 * hh + 64)
                        for qc in range(QC):
                            brel[7].acquire(*Ts)
                            W(Ts, "act", sv)
                            ins = t.matmul(bank(7)[0:1, :], ones_bf[rows, 0:1],
                                           qsq[rows, qc * 512:(qc + 1) * 512],
                                           start=True, stop=True)
                            mv = ctx.inc("pe", ins)
                            W(Vs, "pe", mv)
                            dv = ctx.inc("dve", v.reduce_max(
                                mrow[0:1, 12 + qc:13 + qc], bank(7)[0:1, :], axis=AX.X))
                            brel[7].release("dve", dv)
                        W(Vs, "dve", ctx.val("dve"))
                        ctx.inc("dve", v.reduce_max(mrow[0:1, 10:11], mrow[0:1, 12:16],
                                                    axis=AX.X))
                        W(Vs, "dve", ctx.val("dve"))
                        ctx.inc("dve", v.tensor_add(arow[0:1, hh:hh + 1], mrow[0:1, 10:11],
                                                    mrow[0:1, 8:9]))
                        W(Vs, "dve", ctx.val("dve"))
                        dvb2 = ctx.inc("dve", v.tensor_scalar_mul(
                            arow[0:1, hh:hh + 1], arow[0:1, hh:hh + 1], -1.0 / 16.0))
                        brel[7].acquire(*Ts)
                        W(Ts, "dve", dvb2)
                        ins = t.matmul(PS[:, 7 * 512:7 * 512 + 1], ones_f,
                                       arow[0:1, hh:hh + 1], start=True, stop=True)
                        mv = ctx.inc("pe", ins)
                        W(Vs, "pe", mv)
                        W(Vs, "act", ctx.val("act"))
                        dv = ctx.inc("dve", v.tensor_copy(bias_hd[:, hh:hh + 1],
                                                          PS[:, 7 * 512:7 * 512 + 1]))
                        brel[7].release("dve", dv)
                    qsq_rel.release("pe", ctx.val("pe"))
                    bias_done = ctx.val("dve")

                    for qc in range(QC):
                        qcol = slice(hp * T + qc * 512, hp * T + (qc + 1) * 512)
                        brel[4].acquire(*Ts)
                        brel[5].acquire(*Ts)
                        W(Ts, "dve", p1_dve)
                        W(Ts, "act", p1_act)
                        pv = 0
                        for kt in range(KT):
                            sb_a, sb_b = kt % 2, 2 + kt % 2
                            pt_a, pt_b = kt % 2, 2 + kt % 2
                            brel[sb_a].acquire(*Ts)
                            brel[sb_b].acquire(*Ts)
                            t.matmul(bank(sb_a), K2[0:64, kt * 128:(kt + 1) * 128],
                                     Q[0:64, qcol], start=True, stop=True,
                                     tile_position=(0, 0))
                            ins = t.matmul(bank(sb_b), K2[64:128, kt * 128:(kt + 1) * 128],
                                           Q[64:128, qcol], start=True, stop=True,
                                           tile_position=(64, 0))
                            mv = ctx.inc("pe", ins)
                            PT_rel[pt_a].acquire(*Ss)
                            PT_rel[pt_b].acquire(*Ss)
                            W(Ss, "pe", mv)
                            W(Ss, "dve", bias_done)
                            s.activation(PT[pt_a], bank(sb_a), AF.Exp,
                                         bias=bias_hd[:, 0:1], scale=0.125)
                            ev = ctx.inc("act", s.activation(PT[pt_b], bank(sb_b), AF.Exp,
                                                             bias=bias_hd[:, 1:2],
                                                             scale=0.125))
                            brel[sb_a].release("act", ev)
                            brel[sb_b].release("act", ev)
                            W(Ts, "act", ev)
                            t.matmul(bank(4)[0:65, :], VA[:, kt * 65:(kt + 1) * 65],
                                     PT[pt_a], start=(kt == 0), stop=(kt == KT - 1))
                            pv = ctx.inc("pe", t.matmul(
                                bank(5)[0:65, :], VA[:, kt * 65:(kt + 1) * 65], PT[pt_b],
                                start=(kt == 0), stop=(kt == KT - 1)))
                            PT_rel[pt_a].release("pe", pv)
                            PT_rel[pt_b].release("pe", pv)
                        for hh in range(2):
                            ab = 4 + hh
                            rows = slice(64 * hh, 64 * hh + 64)
                            W(Vs, "pe", pv)
                            dr_ = ctx.inc("dve", v.reciprocal(
                                rdrow[0:1, hh * 512:(hh + 1) * 512],
                                PS[64:65, ab * 512:(ab + 1) * 512]))
                            brel[6].acquire(*Ts)
                            W(Ts, "dve", dr_)
                            ins = t.matmul(PS[rows, 6 * 512:7 * 512], ones_f[0:1, 0:64],
                                           rdrow[0:1, hh * 512:(hh + 1) * 512],
                                           start=True, stop=True)
                            mv = ctx.inc("pe", ins)
                            W(Ss, "pe", mv)
                            sc = ctx.inc("act", s.activation(
                                att_s[rows, hh * 512:(hh + 1) * 512],
                                bank(ab)[0:64, :], AF.Copy))
                            brel[ab].release("act", sc)
                            W(Vs, "act", sc)
                            W(Vs, "pe", mv)
                            dv1 = ctx.inc("dve", v.tensor_tensor(
                                att_s[rows, hh * 512:(hh + 1) * 512],
                                att_s[rows, hh * 512:(hh + 1) * 512],
                                PS[rows, 6 * 512:7 * 512], ALU.mult))
                            brel[6].release("dve", dv1)
                            W(Vs, "dve", dv1)
                            W(Vs, "qxr", xr_load)
                            ctx.inc("dve", v.tensor_tensor(
                                XR[rows, hp * T + qc * 512: hp * T + (qc + 1) * 512],
                                XR[rows, hp * T + qc * 512: hp * T + (qc + 1) * 512],
                                att_s[rows, hh * 512:(hh + 1) * 512], ALU.add))
                ctx.barrier(ALL_ENG)

            tap("X2", [128, 8 * T], XR, "dve")
            ctx.barrier(ALL_ENG)

        with ExitStack() as fb:
            SBB = lambda n, sh, dt=F32: fb.enter_context(nc.sbuf_tensor(n, sh, dt)).ap()
            ln2 = SBB("ln2", [128, 8 * T], BF16)
            comb = SBB("comb", [7, T])
            crow = SBB("crow", [1, T])
            crow_rel = Rel(ctx)
            dcp = 0

            with ExitStack() as fb3:
                SB3 = lambda n, sh, dt=F32: fb3.enter_context(nc.sbuf_tensor(n, sh, dt)).ap()
                xbf = [SB3(f"xbf3_{i}", [128, 8 * 512], BF16) for i in range(2)]
                xbf_rel = [Rel(ctx) for _ in range(2)]
                xsq = [SB3(f"xsq3_{i}", [128, 8 * 512], BF16) for i in range(2)]
                xsq_rel = [Rel(ctx) for _ in range(2)]
                nrm = [SB3(f"nrm3_{i}", [128, 512]) for i in range(2)]
                nrm_rel = [Rel(ctx) for _ in range(2)]
                lg = SB3("lg", [7, T])
                lgT = SB3("lgT", [128, 16 * 7])
                st1 = SB3("st1", [128, 16]); st2 = SB3("st2", [128, 16])
                st3 = SB3("st3", [128, 16]); st4 = SB3("st4", [128, 16])
                pr = SB3("pr", [128, 112]); eq1 = SB3("eq1", [128, 112])
                eq2 = SB3("eq2", [128, 112]); combT = SB3("combT", [128, 112])

                p2_dve = ctx.val("dve")
                for j in range(QC):
                    bi = j % 2
                    xr_sl = XR.rearrange("p (c n) -> p c n", c=8)[:, :, j * 512:(j + 1) * 512]
                    xbf_rel[bi].acquire(*Gs)
                    W(Gs, "dve", p2_dve)
                    cv = ctx.inc("pool", g.tensor_copy(
                        xbf[bi].rearrange("p (c n) -> p c n", c=8), xr_sl))
                    xsq_rel[bi].acquire(*Ss)
                    W(Ss, "dve", p2_dve)
                    sv = ctx.inc("act", s.activation(
                        xsq[bi].rearrange("p (c n) -> p c n", c=8), xr_sl, AF.Square))
                    mvb = ln_stats(xbf[bi], xsq[bi], cv, sv, 512)
                    xbf_rel[bi].release("pe", mvb)
                    xsq_rel[bi].release("pe", mvb)
                    W(Vs, "pe", mvb)
                    lv = 0
                    for c8 in range(8):
                        ni = c8 % 2
                        nrm_rel[ni].acquire(*Vs)
                        dvt = ctx.inc("dve", v.tensor_tensor(
                            nrm[ni], XR[:, c8 * T + j * 512: c8 * T + (j + 1) * 512],
                            bank(1), ALU.mult))
                        W(Vs, "dve", dvt)
                        lv = ctx.inc("dve", v.tensor_tensor(
                            ln2[:, c8 * T + j * 512: c8 * T + (j + 1) * 512],
                            nrm[ni], bank(2), ALU.add))
                        nrm_rel[ni].release("dve", lv)
                    brel[1].release("dve", lv)
                    brel[2].release("dve", lv)
                    brel[3].acquire(*Ts)
                    W(Ts, "pool", wr_done)
                    W(Ts, "dve", lv)
                    ins = None
                    for c8 in range(8):
                        ins = t.matmul(bank(3)[0:7, :], wr_b[:, c8 * 7:(c8 + 1) * 7],
                                       ln2[:, c8 * T + j * 512: c8 * T + (j + 1) * 512],
                                       start=(c8 == 0), stop=(c8 == 7))
                    mlg = ctx.inc("pe", ins)
                    W(Ss, "pe", mlg)
                    slg = ctx.inc("act", s.activation(lg[0:7, j * 512:(j + 1) * 512],
                                                      bank(3)[0:7, :], AF.Copy))
                    brel[3].release("act", slg)
                ln2_done = ctx.val("dve")
                lg_done = ctx.val("act")
                tap("ln2", [128, 4 * T], ln2.bitcast(F32), "dve")
                tap("lg", [7, T], lg, "act")

                brel[4].acquire(*Ts)
                W(Ts, "act", lg_done)
                W(Ts, "pool", ident_done)
                for tt in range(16):
                    ins = t.transpose(PS[:, 4 * 512 + tt * 7: 4 * 512 + (tt + 1) * 7],
                                      lg[0:7, tt * 128:(tt + 1) * 128], ident[0:7, 0:7])
                mtr = ctx.inc("pe", ins)
                W(Vs, "pe", mtr)
                dtr = ctx.inc("dve", v.tensor_copy(lgT, PS[:, 4 * 512: 4 * 512 + 112]))
                brel[4].release("dve", dtr)

                v3 = lambda a: a.rearrange("p (t j) -> p t j", j=7)
                bc = lambda a: a.unsqueeze(2).broadcast_to([128, 16, 7])

                def chain(ins):
                    r = ctx.inc("dve", ins)
                    W(Vs, "dve", r)
                    return r
                W(Vs, "dve", dtr)
                chain(v.reduce_max(st1, v3(lgT), axis=AX.X))
                chain(v.tensor_tensor(v3(pr), v3(lgT), bc(st1), ALU.subtract))
                W(Ss, "dve", ctx.val("dve"))
                spr = ctx.inc("act", s.activation(pr, pr, AF.Exp))
                W(Vs, "act", spr)
                chain(v.reduce_sum(st2, v3(pr), axis=AX.X))
                chain(v.reciprocal(st2, st2))
                chain(v.tensor_tensor(v3(pr), v3(pr), bc(st2), ALU.mult))
                chain(v.reduce_max(st1, v3(pr), axis=AX.X))
                chain(v.tensor_tensor(v3(eq1), v3(pr), bc(st1), ALU.is_equal))
                chain(v.tensor_scalar(eq2, eq1, -1.0, 1.0, ALU.mult, ALU.add))
                chain(v.tensor_tensor(pr, pr, eq2, ALU.mult))
                chain(v.reduce_max(st2, v3(pr), axis=AX.X))
                chain(v.tensor_tensor(v3(eq2), v3(pr), bc(st2), ALU.is_equal))
                chain(v.tensor_tensor(st3, st1, st2, ALU.add))
                chain(v.tensor_scalar_add(st3, st3, 1e-9))
                chain(v.reciprocal(st3, st3))
                chain(v.tensor_tensor(st4, st1, st3, ALU.mult))
                chain(v.tensor_tensor(st3, st2, st3, ALU.mult))
                chain(v.tensor_tensor(v3(eq1), v3(eq1), bc(st4), ALU.mult))
                chain(v.tensor_tensor(v3(eq2), v3(eq2), bc(st3), ALU.mult))
                dcomb = ctx.inc("dve", v.tensor_tensor(combT, eq1, eq2, ALU.add))

                for b4 in range(4, 8):
                    brel[b4].acquire(*Ts)
                W(Ts, "dve", dcomb)
                for tt in range(16):
                    ins = t.transpose(PS[0:7, 2048 + tt * 128: 2048 + (tt + 1) * 128],
                                      combT[:, tt * 7:(tt + 1) * 7], ident)
                mtr2 = ctx.inc("pe", ins)
                W(Vs, "pe", mtr2)
                dcp = ctx.inc("dve", v.tensor_copy(comb, PS[0:7, 2048:4096]))
                for b4 in range(4, 8):
                    brel[b4].release("dve", dcp)
                tap("comb", [7, T], comb, "dve")
                ctx.barrier(ALL_ENG)

            with ExitStack() as fb4:
                SB4 = lambda n, sh, dt=F32: fb4.enter_context(nc.sbuf_tensor(n, sh, dt)).ap()
                silg = SB4("silg", [128, T])
                silg_rel = Rel(ctx)
                hsb = SB4("hsb", [128, 8 * T], BF16)
                hsb_rel = [Rel(ctx) for _ in range(8)]
                cbs = SB4("cbs", [128, T])
                cbs_rel = Rel(ctx)
                wgb = [SB4(f"wgb{i}", [128, 8 * 128], BF16) for i in range(2)]
                wgb_rel = [Rel(ctx) for _ in range(2)]
                wub = [SB4(f"wub{i}", [128, 8 * 128], BF16) for i in range(2)]
                wub_rel = [Rel(ctx) for _ in range(2)]
                wdb = [SB4(f"wdb{i}", [128, 8 * 128], BF16) for i in range(2)]
                wdb_rel = [Rel(ctx) for _ in range(2)]

                ln2_doneP = ctx.val("dve")
                xr_last = [[ctx.val("dve")] * QC for _ in range(8)]
                for e in range(8):
                    if e > 0:
                        crow_rel.acquire(*Gs)
                        W(Gs, "dve", dcp)
                        cr = ctx.inc("qg", g.dma_start(crow, comb[e - 1:e, :]))
                        cbs_rel.acquire(*Vs)
                        for qc in range(QC):
                            bk = 2 + qc % 2
                            brel[bk].acquire(*Ts)
                            W(Ts, "qg", cr)
                            ins = t.matmul(bank(bk), ones_f,
                                           crow[0:1, qc * 512:(qc + 1) * 512],
                                           start=True, stop=True)
                            mv = ctx.inc("pe", ins)
                            W(Vs, "pe", mv)
                            dv = ctx.inc("dve", v.tensor_copy(
                                cbs[:, qc * 512:(qc + 1) * 512], bank(bk)))
                            brel[bk].release("dve", dv)
                        crow_rel.release("pe", ctx.val("pe"))
                        cb_done = ctx.val("dve")
                    else:
                        cb_done = 0

                    for db in range(4):
                        h_done = 0
                        for pc in range(8):
                            wi = pc % 2
                            wgb_rel[wi].acquire(*Gs)
                            gv = stage_cast(gate_t[e, db, pc], wgb[wi], 1024)
                            wub_rel[wi].acquire(*Gs)
                            uv = stage_cast(up_t[e, db, pc], wub[wi], 1024)
                            for qc in range(QC):
                                brel[qc].acquire(*Ts)
                            W(Ts, "pool", gv)
                            W(Ts, "dve", ln2_doneP)
                            ins = None
                            for qc in range(QC):
                                for c8 in range(8):
                                    ins = t.matmul(
                                        bank(qc), wgb[wi][:, c8 * 128:(c8 + 1) * 128],
                                        ln2[:, c8 * T + qc * 512: c8 * T + (qc + 1) * 512],
                                        start=(c8 == 0), stop=(c8 == 7))
                            mg = ctx.inc("pe", ins)
                            for qc in range(QC):
                                brel[4 + qc].acquire(*Ts)
                            W(Ts, "pool", uv)
                            for qc in range(QC):
                                for c8 in range(8):
                                    ins = t.matmul(
                                        bank(4 + qc), wub[wi][:, c8 * 128:(c8 + 1) * 128],
                                        ln2[:, c8 * T + qc * 512: c8 * T + (qc + 1) * 512],
                                        start=(c8 == 0), stop=(c8 == 7))
                            mu = ctx.inc("pe", ins)
                            wgb_rel[wi].release("pe", mg)
                            wub_rel[wi].release("pe", mu)
                            silg_rel.acquire(*Ss)
                            W(Ss, "pe", mg)
                            ssg = ctx.inc("act", s.activation(silg, PS[:, 0:2048], AF.Silu))
                            for qc in range(QC):
                                brel[qc].release("act", ssg)
                            hsb_rel[pc].acquire(*Vs)
                            W(Vs, "act", ssg)
                            W(Vs, "pe", mu)
                            if e > 0:
                                W(Vs, "dve", cb_done)
                                ctx.inc("dve", v.tensor_tensor(silg, PS[:, 2048:4096],
                                                               silg, ALU.mult))
                                W(Vs, "dve", ctx.val("dve"))
                                hv = ctx.inc("dve", v.tensor_tensor(
                                    hsb[:, pc * T:(pc + 1) * T], silg, cbs, ALU.mult))
                            else:
                                hv = ctx.inc("dve", v.tensor_tensor(
                                    hsb[:, pc * T:(pc + 1) * T], PS[:, 2048:4096],
                                    silg, ALU.mult))
                            silg_rel.release("dve", hv)
                            for qc in range(QC):
                                brel[4 + qc].release("dve", hv)
                            h_done = hv
                        for oc in range(8):
                            wi = oc % 2
                            wdb_rel[wi].acquire(*Gs)
                            dv_w = stage_cast(down_t[e, db, oc], wdb[wi], 1024)
                            for qc in range(QC):
                                bk = qc
                                brel[bk].acquire(*Ts)
                                W(Ts, "pool", dv_w)
                                W(Ts, "dve", h_done)
                                ins = None
                                for pcx in range(8):
                                    ins = t.matmul(
                                        bank(bk), wdb[wi][:, pcx * 128:(pcx + 1) * 128],
                                        hsb[:, pcx * T + qc * 512: pcx * T + (qc + 1) * 512],
                                        start=(pcx == 0), stop=(pcx == 7))
                                md = ctx.inc("pe", ins)
                                W(Vs, "pe", md)
                                W(Vs, "dve", xr_last[oc][qc])
                                dvx = ctx.inc("dve", v.tensor_tensor(
                                    XR[:, oc * T + qc * 512: oc * T + (qc + 1) * 512],
                                    XR[:, oc * T + qc * 512: oc * T + (qc + 1) * 512],
                                    bank(bk), ALU.add))
                                xr_last[oc][qc] = dvx
                                brel[bk].release("dve", dvx)
                            wdb_rel[wi].release("pe", ctx.val("pe"))
                        pe_after_down = ctx.val("pe")
                        for pc in range(8):
                            hsb_rel[pc].release("pe", pe_after_down)
                    cbs_rel.release("dve", ctx.val("dve"))
                ctx.barrier(ALL_ENG)

        W(Qs, "dve", ctx.val("dve"))
        ctx.inc("st", sync.dma_start(
            out.rearrange("(c p) n -> p c n", p=128),
            XR.rearrange("p (c n) -> p c n", c=8)))
        sync.wait_ge(ctx.sems["st"][0], ctx.val("st"))
    return nc, tap_aps



def host_inputs(inputs, core):
    b, half = core // 2, core % 2
    x = np.asarray(inputs["L0_attn_x"])
    xT = np.ascontiguousarray(x[b].T)
    x_own = np.ascontiguousarray(xT[:, half * T:(half + 1) * T])

    def tile_in_by_out(w):
        wt = np.ascontiguousarray(np.asarray(w).T)
        o = w.shape[0]
        return np.ascontiguousarray(
            wt.reshape(8, 128, o).transpose(1, 0, 2).reshape(128, 8 * o))

    wq_h = tile_in_by_out(inputs["L0_attn_Wq"])
    wk_h = tile_in_by_out(inputs["L0_attn_Wk"])
    wv_h = tile_in_by_out(inputs["L0_attn_Wv"])
    wr_h = tile_in_by_out(inputs["L0_ffn_router_w"])

    def tile_gateup(w):
        wt = np.asarray(w).T
        r = wt.reshape(8, 128, 4, 8, 128)
        return np.ascontiguousarray(r.transpose(2, 3, 1, 0, 4).reshape(4, 8, 128, 1024))

    def tile_down(w):
        wt = np.asarray(w).T
        r = wt.reshape(4, 8, 128, 8, 128)
        return np.ascontiguousarray(r.transpose(0, 3, 2, 1, 4).reshape(4, 8, 128, 1024))

    gate_h = np.stack([tile_gateup(inputs["L0_ffn_shared_gate"])]
                      + [tile_gateup(np.asarray(inputs["L0_ffn_routed_gate"])[i])
                         for i in range(7)])
    up_h = np.stack([tile_gateup(inputs["L0_ffn_shared_up"])]
                    + [tile_gateup(np.asarray(inputs["L0_ffn_routed_up"])[i])
                       for i in range(7)])
    down_h = np.stack([tile_down(inputs["L0_ffn_shared_down"])]
                      + [tile_down(np.asarray(inputs["L0_ffn_routed_down"])[i])
                         for i in range(7)])

    return dict(x_full=xT, x_own=x_own, wq=wq_h, wk=wk_h, wv=wv_h, wr=wr_h,
                gate_t=gate_h, up_t=up_h, down_t=down_h)


def kernel(**inputs):
    nc, _ = build()
    shared = None
    in_maps = []
    for c in range(8):
        m = host_inputs(inputs, c)
        if shared is None:
            shared = {k: m[k] for k in ("wq", "wk", "wv", "wr", "gate_t", "up_t", "down_t")}
        else:
            for k in shared:
                m[k] = shared[k]
        in_maps.append(m)
    res = run_bass_kernel_spmd(nc, in_maps, core_ids=list(range(8)))
    outT = np.zeros((4, D, S), dtype=np.float32)
    for c in range(8):
        b, half = c // 2, c % 2
        outT[b][:, half * T:(half + 1) * T] = res.results[c]["out"]
    return np.ascontiguousarray(outT.transpose(0, 2, 1))


# version 03
# speedup vs baseline: 1.1989x; 1.1989x over previous
import numpy as np
from contextlib import ExitStack

import concourse.bass as bass
import concourse.mybir as mybir
from concourse.bass_utils import run_bass_kernel_spmd

F32 = mybir.dt.float32
BF16 = mybir.dt.bfloat16
AF = mybir.ActivationFunctionType
ALU = mybir.AluOpType
AX = mybir.AxisListType

D = 1024
S = 4096
T = 2048
NP = 8
EPS = 1e-5
KT = S // 128
QC = T // 512
CW = 256


class Ctx:
    def __init__(self, nc, stack):
        self.nc = nc
        self.stack = stack
        self.sems = {}
        self.hwm = {}

    def sem(self, name, step=1):
        h = self.stack.enter_context(self.nc.semaphore(name))
        self.sems[name] = [h, step, 0]

    def inc(self, name, inst):
        e = self.sems[name]
        inst.then_inc(e[0], e[1])
        e[2] += e[1]
        return e[2]

    def val(self, name):
        return self.sems[name][2]

    def wait(self, ekey, eng, name, v):
        if v <= 0:
            return
        key = (ekey, name)
        if self.hwm.get(key, 0) >= v:
            return
        eng.wait_ge(self.sems[name][0], v)
        self.hwm[key] = v

    def barrier(self, engines):
        for ekey, eng in engines:
            for name in self.sems:
                self.wait(ekey, eng, name, self.val(name))


class Rel:
    def __init__(self, ctx):
        self.ctx = ctx
        self.rel = []

    def acquire(self, ekey, eng):
        for (sn, vv) in self.rel:
            self.ctx.wait(ekey, eng, sn, vv)
        self.rel = []

    def release(self, sem_name, val):
        self.rel.append((sem_name, val))


def build(taps=()):
    nc = bass.Bass("TRN2", target_bir_lowering=False)
    dr = lambda n, sh, kind: nc.dram_tensor(n, sh, F32, kind=kind).ap()

    x_full = dr("x_full", [D, S], "ExternalInput")
    x_own = dr("x_own", [D, T], "ExternalInput")
    wq = dr("wq", [128, 8 * 1024], "ExternalInput")
    wk = dr("wk", [128, 8 * 64], "ExternalInput")
    wv = dr("wv", [128, 8 * 64], "ExternalInput")
    wr = dr("wr", [128, 8 * 7], "ExternalInput")
    gate_t = dr("gate_t", [8, 4, 8, 128, 1024], "ExternalInput")
    up_t = dr("up_t", [8, 4, 8, 128, 1024], "ExternalInput")
    down_t = dr("down_t", [8, 4, 8, 128, 1024], "ExternalInput")
    out = dr("out", [D, T], "ExternalOutput")

    tap_aps = {}

    with ExitStack() as stack:
        ctx = Ctx(nc, stack)
        SB = lambda n, sh, dt=F32: stack.enter_context(nc.sbuf_tensor(n, sh, dt)).ap()
        sync, v, s, t, g = nc.sync, nc.vector, nc.scalar, nc.tensor, nc.gpsimd
        Ts, Ss, Vs, Gs, Qs = ("t", t), ("s", s), ("v", v), ("g", g), ("q", sync)
        ALL_ENG = [Ts, Ss, Vs, Gs, Qs]

        for name, step in ([("qg", 16), ("qxr", 16), ("pe", 1), ("act", 1),
                            ("dve", 1), ("pool", 1), ("st", 16)]
                           + [(f"qstg{i}", 16) for i in range(4)]
                           + [(f"qxf{i}", 16) for i in range(2)]):
            ctx.sem(name, step)
        W = lambda ee, name, val: ctx.wait(ee[0], ee[1], name, val)

        def tap(name, shape, src_ap, dep_sem):
            if name not in taps:
                return
            ap = dr("tap_" + name, shape, "ExternalOutput")
            tap_aps[name] = ap
            W(Qs, dep_sem, ctx.val(dep_sem))
            ctx.inc("st", sync.dma_start(ap, src_ap))

        XR = SB("XR", [128, 8 * T])
        PS = stack.enter_context(nc.psum_tensor("PS", [128, 4096], F32)).ap()
        bank = lambda i: PS[:, i * 512:(i + 1) * 512]
        brel = [Rel(ctx) for _ in range(8)]

        wk_b = SB("wk_b", [128, 8 * 64], BF16)
        wv_b = SB("wv_b", [128, 8 * 64], BF16)
        wr_b = SB("wr_b", [128, 8 * 7], BF16)
        ones_bf = SB("ones_bf", [128, 2], BF16)
        ones_f = SB("ones_f", [1, 128])
        eps_t = SB("eps_t", [1, 2])
        ident = SB("ident", [128, 128])
        srow = SB("srow", [1, 6 * 512])
        mrow = SB("mrow", [1, 16])
        arow = SB("arow", [1, 16])

        stg = [SB(f"stg{i}", [128, 8 * 128]) for i in range(4)]
        stg_rel = [Rel(ctx) for _ in range(4)]
        stg_i = [0]

        def stage_cast(src_ap, dst_bf_ap, n_cols):
            i = stg_i[0] % 4
            stg_i[0] += 1
            stg_rel[i].acquire(*Qs)
            dv = ctx.inc(f"qstg{i}", sync.dma_start(stg[i][:, 0:n_cols], src_ap))
            W(Gs, f"qstg{i}", dv)
            pv = ctx.inc("pool", g.tensor_copy(dst_bf_ap, stg[i][:, 0:n_cols]))
            stg_rel[i].release("pool", pv)
            return pv

        ctx.inc("dve", v.memset(ones_bf, 1.0))
        ctx.inc("dve", v.memset(ones_f, 1.0))
        consts_dve = ctx.inc("dve", v.memset(eps_t, EPS))
        ctx.inc("pool", g.memset(ident, 1.0))
        W(Gs, "pool", ctx.val("pool"))
        ident_done = ctx.inc("pool", g.affine_select(
            ident, ident, [[1, 128]], ALU.is_equal, 0.0, base=0, channel_multiplier=-1))

        xr_load = ctx.inc("qxr", sync.dma_start(
            XR.rearrange("p (c n) -> p c n", c=8),
            x_own.rearrange("(c p) n -> p c n", p=128)))

        wk_done = stage_cast(wk, wk_b, 8 * 64)
        wv_done = stage_cast(wv, wv_b, 8 * 64)
        wr_done = stage_cast(wr, wr_b, 8 * 7)

        def ln_stats(xbf_ap, xsq_ap, ready_pool, ready_act, cw):
            brel[0].acquire(*Ts)
            W(Ts, "dve", consts_dve)
            W(Ts, "pool", ready_pool)
            W(Ts, "act", ready_act)
            ins = None
            for c8 in range(8):
                ins = t.matmul(bank(0)[0:1, 0:cw], ones_bf[:, 0:1],
                               xbf_ap[:, c8 * cw:(c8 + 1) * cw],
                               start=(c8 == 0), stop=(c8 == 7))
            mv_sum = ctx.inc("pe", ins)
            brel[1].acquire(*Ts)
            for c8 in range(8):
                ins = t.matmul(bank(1)[0:1, 0:cw], ones_bf[:, 0:1],
                               xsq_ap[:, c8 * cw:(c8 + 1) * cw],
                               start=(c8 == 0), stop=(c8 == 7))
            mv_sq = ctx.inc("pe", ins)
            W(Vs, "pe", mv_sq)
            dv1 = ctx.inc("dve", v.tensor_scalar_mul(srow[0:1, 0:cw], bank(0)[0:1, 0:cw],
                                                     1.0 / D))
            dv2 = ctx.inc("dve", v.tensor_scalar_mul(srow[0:1, 512:512 + cw],
                                                     bank(1)[0:1, 0:cw], 1.0 / D))
            brel[0].release("dve", dv1)
            W(Vs, "dve", dv2)
            ctx.inc("dve", v.tensor_mul(srow[0:1, 1024:1024 + cw], srow[0:1, 0:cw],
                                        srow[0:1, 0:cw]))
            W(Vs, "dve", ctx.val("dve"))
            dvv = ctx.inc("dve", v.tensor_sub(srow[0:1, 1024:1024 + cw],
                                              srow[0:1, 512:512 + cw],
                                              srow[0:1, 1024:1024 + cw]))
            brel[1].release("dve", dvv)
            W(Ss, "dve", dvv)
            W(Ss, "dve", consts_dve)
            ctx.inc("act", s.activation(srow[0:1, 1536:1536 + cw], srow[0:1, 1024:1024 + cw],
                                        AF.Ln, bias=eps_t[0:1, 0:1], scale=1.0))
            W(Ss, "act", ctx.val("act"))
            sv2 = ctx.inc("act", s.activation(srow[0:1, 1536:1536 + cw],
                                              srow[0:1, 1536:1536 + cw], AF.Exp, scale=-0.5))
            W(Vs, "act", sv2)
            dvn = ctx.inc("dve", v.scalar_tensor_tensor(
                srow[0:1, 2048:2048 + cw], srow[0:1, 0:cw], -1.0, srow[0:1, 1536:1536 + cw],
                ALU.mult, ALU.mult))
            brel[1].acquire(*Ts)
            brel[2].acquire(*Ts)
            W(Ts, "act", sv2)
            W(Ts, "dve", dvn)
            t.matmul(bank(1)[:, 0:cw], ones_f, srow[0:1, 1536:1536 + cw],
                     start=True, stop=True)
            ins = t.matmul(bank(2)[:, 0:cw], ones_f, srow[0:1, 2048:2048 + cw],
                           start=True, stop=True)
            return ctx.inc("pe", ins)

        with ExitStack() as fa:
            SBA = lambda n, sh, dt=F32: fa.enter_context(nc.sbuf_tensor(n, sh, dt)).ap()
            Q = SBA("Q", [128, 8 * T], BF16)
            K2 = SBA("K2", [128, S], BF16)
            VA = SBA("VA", [128, KT * 65], BF16)
            va_ones = ctx.inc("dve", v.memset(VA, 1.0))

            with ExitStack() as fa1:
                fa1.enter_context(nc.named_scope("P1"))
                SB1 = lambda n, sh, dt=F32: fa1.enter_context(nc.sbuf_tensor(n, sh, dt)).ap()
                wq_b = SB1("wq_b", [128, 8 * 1024], BF16)
                xf = [SB1(f"xf{i}", [128, 8 * CW]) for i in range(2)]
                xf_rel = [Rel(ctx) for _ in range(2)]
                xbf = [SB1(f"xbf{i}", [128, 8 * CW], BF16) for i in range(2)]
                xbf_rel = [Rel(ctx) for _ in range(2)]
                xsq = [SB1(f"xsq{i}", [128, 8 * CW], BF16) for i in range(2)]
                xsq_rel = [Rel(ctx) for _ in range(2)]
                lnb = [SB1(f"lnb{i}", [128, 8 * CW], BF16) for i in range(2)]
                lnb_rel = [Rel(ctx) for _ in range(2)]
                nrm = [SB1(f"nrm{i}", [128, CW]) for i in range(2)]
                nrm_rel = [Rel(ctx) for _ in range(2)]

                wq_done = 0
                for c8 in range(8):
                    wq_done = stage_cast(wq[:, c8 * 1024:(c8 + 1) * 1024],
                                         wq_b[:, c8 * 1024:(c8 + 1) * 1024], 1024)

                def ln_chunk(src_dram, bi, ln_out):
                    xf_rel[bi].acquire(*Qs)
                    dv = ctx.inc(f"qxf{bi}", sync.dma_start(
                        xf[bi].rearrange("p (c n) -> p c n", c=8), src_dram))
                    W(Gs, f"qxf{bi}", dv)
                    xbf_rel[bi].acquire(*Gs)
                    cv = ctx.inc("pool", g.tensor_copy(xbf[bi], xf[bi]))
                    W(Ss, f"qxf{bi}", dv)
                    xsq_rel[bi].acquire(*Ss)
                    sv = ctx.inc("act", s.activation(xsq[bi], xf[bi], AF.Square))
                    mvb = ln_stats(xbf[bi], xsq[bi], cv, sv, CW)
                    xbf_rel[bi].release("pe", mvb)
                    xsq_rel[bi].release("pe", mvb)
                    W(Vs, "pe", mvb)
                    lv = 0
                    for c8 in range(8):
                        ni = c8 % 2
                        nrm_rel[ni].acquire(*Vs)
                        dvt = ctx.inc("dve", v.tensor_tensor(
                            nrm[ni], xf[bi][:, c8 * CW:(c8 + 1) * CW],
                            bank(1)[:, 0:CW], ALU.mult))
                        W(Vs, "dve", dvt)
                        lv = ctx.inc("dve", v.tensor_tensor(
                            ln_out[:, c8 * CW:(c8 + 1) * CW], nrm[ni],
                            bank(2)[:, 0:CW], ALU.add))
                        nrm_rel[ni].release("dve", lv)
                    xf_rel[bi].release("dve", lv)
                    brel[1].release("dve", lv)
                    brel[2].release("dve", lv)
                    return lv

                xr_src = x_full.rearrange("(c p) n -> p c n", p=128)
                for j in range(S // CW):
                    bi = j % 2
                    lnb_rel[bi].acquire(*Vs)
                    lv = ln_chunk(xr_src[:, :, j * CW:(j + 1) * CW], bi, lnb[bi])
                    brel[3].acquire(*Ts)
                    W(Ts, "pool", wk_done)
                    W(Ts, "dve", lv)
                    ins = None
                    for c8 in range(8):
                        ins = t.matmul(bank(3)[0:64, 0:CW], wk_b[:, c8 * 64:(c8 + 1) * 64],
                                       lnb[bi][:, c8 * CW:(c8 + 1) * CW],
                                       start=(c8 == 0), stop=(c8 == 7))
                    mk = ctx.inc("pe", ins)
                    W(Ss, "pe", mk)
                    s.activation(K2[0:64, j * CW:(j + 1) * CW], bank(3)[0:64, 0:CW], AF.Copy)
                    se = ctx.inc("act", s.activation(K2[64:128, j * CW:(j + 1) * CW],
                                                     bank(3)[0:64, 0:CW], AF.Copy))
                    brel[3].release("act", se)
                    brel[4].acquire(*Ts)
                    W(Ts, "pool", wv_done)
                    ins = None
                    for kt4 in range(CW // 128):
                        for c8 in range(8):
                            ins = t.matmul(
                                PS[:, 4 * 512 + kt4 * 64: 4 * 512 + (kt4 + 1) * 64],
                                lnb[bi][:, c8 * CW + kt4 * 128: c8 * CW + (kt4 + 1) * 128],
                                wv_b[:, c8 * 64:(c8 + 1) * 64],
                                start=(c8 == 0), stop=(c8 == 7))
                    mvv = ctx.inc("pe", ins)
                    W(Vs, "pe", mvv)
                    W(Vs, "dve", va_ones)
                    kt0 = j * (CW // 128)
                    dvv2 = ctx.inc("dve", v.tensor_copy(
                        VA[:, kt0 * 65:(kt0 + CW // 128) * 65]
                        .rearrange("p (k n) -> p k n", k=CW // 128)[:, :, 0:64],
                        PS[:, 4 * 512: 4 * 512 + (CW // 128) * 64]
                        .rearrange("p (k n) -> p k n", k=CW // 128)))
                    brel[4].release("dve", dvv2)
                    lnb_rel[bi].release("pe", mvv)

                xo_src = x_own.rearrange("(c p) n -> p c n", p=128)
                for j in range(T // CW):
                    bi = j % 2
                    lnb_rel[bi].acquire(*Vs)
                    lv = ln_chunk(xo_src[:, :, j * CW:(j + 1) * CW], bi, lnb[bi])
                    for hp in range(8):
                        bk = 5 + (hp % 2)
                        brel[bk].acquire(*Ts)
                        W(Ts, "pool", wq_done)
                        W(Ts, "dve", lv)
                        ins = None
                        for c8 in range(8):
                            ins = t.matmul(
                                bank(bk)[:, 0:CW],
                                wq_b[:, c8 * 1024 + hp * 128: c8 * 1024 + (hp + 1) * 128],
                                lnb[bi][:, c8 * CW:(c8 + 1) * CW],
                                start=(c8 == 0), stop=(c8 == 7))
                        mq = ctx.inc("pe", ins)
                        W(Ss, "pe", mq)
                        qe = ctx.inc("act", s.activation(
                            Q[:, hp * T + j * CW: hp * T + (j + 1) * CW],
                            bank(bk)[:, 0:CW], AF.Copy))
                        brel[bk].release("act", qe)
                    lnb_rel[bi].release("pe", ctx.val("pe"))

                tap("lnq", [128, 8 * CW // 2], lnb[(T // CW - 1) % 2].bitcast(F32), "dve")
                ctx.barrier(ALL_ENG)

            p1_act = ctx.val("act")
            p1_dve = ctx.val("dve")
            tap("K2", [128, S // 2], K2.bitcast(F32), "act")
            tap("VA", [128, (KT * 65 + 1) // 2], VA.bitcast(F32), "dve")
            tap("Q", [128, 4 * T], Q.bitcast(F32), "act")

            with ExitStack() as fa2:
                fa2.enter_context(nc.named_scope("P2"))
                SB2 = lambda n, sh, dt=F32: fa2.enter_context(nc.sbuf_tensor(n, sh, dt)).ap()
                ksq = SB2("ksq", [64, S], BF16)
                qsq = SB2("qsq", [128, T], BF16)
                qsq_rel = Rel(ctx)
                PT = [SB2(f"PT{i}", [128, 512], BF16) for i in range(4)]
                PT_rel = [Rel(ctx) for _ in range(4)]
                att_s = SB2("att_s", [128, 2 * 512])
                bias_hd = SB2("bias_hd", [128, 2])
                rdrow = SB2("rdrow", [1, 2 * 512])

                W(Ss, "act", p1_act)
                kq = ctx.inc("act", s.activation(ksq, K2[0:64, :], AF.Square))
                for c in range(8):
                    brel[7].acquire(*Ts)
                    W(Ts, "act", kq)
                    ins = t.matmul(bank(7)[0:1, :], ones_bf[0:64, 0:1],
                                   ksq[:, c * 512:(c + 1) * 512], start=True, stop=True)
                    mv = ctx.inc("pe", ins)
                    W(Vs, "pe", mv)
                    dv = ctx.inc("dve", v.reduce_max(mrow[0:1, c:c + 1], bank(7)[0:1, :],
                                                     axis=AX.X))
                    brel[7].release("dve", dv)
                W(Vs, "dve", ctx.val("dve"))
                ctx.inc("dve", v.reduce_max(mrow[0:1, 8:9], mrow[0:1, 0:8], axis=AX.X))

                for hp in range(NP):
                    qsq_rel.acquire(*Ss)
                    W(Ss, "act", p1_act)
                    sv = ctx.inc("act", s.activation(qsq, Q[:, hp * T:(hp + 1) * T],
                                                     AF.Square))
                    for hh in range(2):
                        rows = slice(64 * hh, 64 * hh + 64)
                        for qc in range(QC):
                            brel[7].acquire(*Ts)
                            W(Ts, "act", sv)
                            ins = t.matmul(bank(7)[0:1, :], ones_bf[rows, 0:1],
                                           qsq[rows, qc * 512:(qc + 1) * 512],
                                           start=True, stop=True)
                            mv = ctx.inc("pe", ins)
                            W(Vs, "pe", mv)
                            dv = ctx.inc("dve", v.reduce_max(
                                mrow[0:1, 12 + qc:13 + qc], bank(7)[0:1, :], axis=AX.X))
                            brel[7].release("dve", dv)
                        W(Vs, "dve", ctx.val("dve"))
                        ctx.inc("dve", v.reduce_max(mrow[0:1, 10:11], mrow[0:1, 12:16],
                                                    axis=AX.X))
                        W(Vs, "dve", ctx.val("dve"))
                        ctx.inc("dve", v.tensor_add(arow[0:1, hh:hh + 1], mrow[0:1, 10:11],
                                                    mrow[0:1, 8:9]))
                        W(Vs, "dve", ctx.val("dve"))
                        dvb2 = ctx.inc("dve", v.tensor_scalar_mul(
                            arow[0:1, hh:hh + 1], arow[0:1, hh:hh + 1], -1.0 / 16.0))
                        brel[7].acquire(*Ts)
                        W(Ts, "dve", dvb2)
                        ins = t.matmul(PS[:, 7 * 512:7 * 512 + 1], ones_f,
                                       arow[0:1, hh:hh + 1], start=True, stop=True)
                        mv = ctx.inc("pe", ins)
                        W(Vs, "pe", mv)
                        W(Vs, "act", ctx.val("act"))
                        dv = ctx.inc("dve", v.tensor_copy(bias_hd[:, hh:hh + 1],
                                                          PS[:, 7 * 512:7 * 512 + 1]))
                        brel[7].release("dve", dv)
                    qsq_rel.release("pe", ctx.val("pe"))
                    bias_done = ctx.val("dve")

                    for qc in range(QC):
                        qcol = slice(hp * T + qc * 512, hp * T + (qc + 1) * 512)
                        brel[4].acquire(*Ts)
                        brel[5].acquire(*Ts)
                        W(Ts, "dve", p1_dve)
                        W(Ts, "act", p1_act)
                        pv = 0
                        exp_of = {}

                        def issue_scores(kt):
                            sb_a, sb_b = kt % 2, 2 + kt % 2
                            brel[sb_a].acquire(*Ts)
                            brel[sb_b].acquire(*Ts)
                            t.matmul(bank(sb_a), K2[0:64, kt * 128:(kt + 1) * 128],
                                     Q[0:64, qcol], start=True, stop=True,
                                     tile_position=(0, 0))
                            ins = t.matmul(bank(sb_b), K2[64:128, kt * 128:(kt + 1) * 128],
                                           Q[64:128, qcol], start=True, stop=True,
                                           tile_position=(64, 0))
                            mv = ctx.inc("pe", ins)
                            pt_a, pt_b = kt % 2, 2 + kt % 2
                            PT_rel[pt_a].acquire(*Ss)
                            PT_rel[pt_b].acquire(*Ss)
                            W(Ss, "pe", mv)
                            W(Ss, "dve", bias_done)
                            s.activation(PT[pt_a], bank(sb_a), AF.Exp,
                                         bias=bias_hd[:, 0:1], scale=0.125)
                            ev = ctx.inc("act", s.activation(PT[pt_b], bank(sb_b), AF.Exp,
                                                             bias=bias_hd[:, 1:2],
                                                             scale=0.125))
                            brel[sb_a].release("act", ev)
                            brel[sb_b].release("act", ev)
                            exp_of[kt] = ev

                        issue_scores(0)
                        for kt in range(KT):
                            if kt + 1 < KT:
                                issue_scores(kt + 1)
                            pt_a, pt_b = kt % 2, 2 + kt % 2
                            W(Ts, "act", exp_of[kt])
                            t.matmul(bank(4)[0:65, :], VA[:, kt * 65:(kt + 1) * 65],
                                     PT[pt_a], start=(kt == 0), stop=(kt == KT - 1))
                            pv = ctx.inc("pe", t.matmul(
                                bank(5)[0:65, :], VA[:, kt * 65:(kt + 1) * 65], PT[pt_b],
                                start=(kt == 0), stop=(kt == KT - 1)))
                            PT_rel[pt_a].release("pe", pv)
                            PT_rel[pt_b].release("pe", pv)
                        for hh in range(2):
                            ab = 4 + hh
                            rows = slice(64 * hh, 64 * hh + 64)
                            W(Vs, "pe", pv)
                            dr_ = ctx.inc("dve", v.reciprocal(
                                rdrow[0:1, hh * 512:(hh + 1) * 512],
                                PS[64:65, ab * 512:(ab + 1) * 512]))
                            brel[6].acquire(*Ts)
                            W(Ts, "dve", dr_)
                            ins = t.matmul(PS[rows, 6 * 512:7 * 512], ones_f[0:1, 0:64],
                                           rdrow[0:1, hh * 512:(hh + 1) * 512],
                                           start=True, stop=True)
                            mv = ctx.inc("pe", ins)
                            W(Ss, "pe", mv)
                            sc = ctx.inc("act", s.activation(
                                att_s[rows, hh * 512:(hh + 1) * 512],
                                bank(ab)[0:64, :], AF.Copy))
                            brel[ab].release("act", sc)
                            W(Vs, "act", sc)
                            W(Vs, "pe", mv)
                            dv1 = ctx.inc("dve", v.tensor_tensor(
                                att_s[rows, hh * 512:(hh + 1) * 512],
                                att_s[rows, hh * 512:(hh + 1) * 512],
                                PS[rows, 6 * 512:7 * 512], ALU.mult))
                            brel[6].release("dve", dv1)
                            W(Vs, "dve", dv1)
                            W(Vs, "qxr", xr_load)
                            ctx.inc("dve", v.tensor_tensor(
                                XR[rows, hp * T + qc * 512: hp * T + (qc + 1) * 512],
                                XR[rows, hp * T + qc * 512: hp * T + (qc + 1) * 512],
                                att_s[rows, hh * 512:(hh + 1) * 512], ALU.add))
                ctx.barrier(ALL_ENG)

            tap("X2", [128, 8 * T], XR, "dve")
            ctx.barrier(ALL_ENG)

        with ExitStack() as fb:
            SBB = lambda n, sh, dt=F32: fb.enter_context(nc.sbuf_tensor(n, sh, dt)).ap()
            ln2 = SBB("ln2", [128, 8 * T], BF16)
            comb = SBB("comb", [7, T])
            crow = SBB("crow", [1, T])
            crow_rel = Rel(ctx)
            dcp = 0

            with ExitStack() as fb3:
                fb3.enter_context(nc.named_scope("P3"))
                SB3 = lambda n, sh, dt=F32: fb3.enter_context(nc.sbuf_tensor(n, sh, dt)).ap()
                xbf = [SB3(f"xbf3_{i}", [128, 8 * 512], BF16) for i in range(2)]
                xbf_rel = [Rel(ctx) for _ in range(2)]
                xsq = [SB3(f"xsq3_{i}", [128, 8 * 512], BF16) for i in range(2)]
                xsq_rel = [Rel(ctx) for _ in range(2)]
                nrm = [SB3(f"nrm3_{i}", [128, 512]) for i in range(2)]
                nrm_rel = [Rel(ctx) for _ in range(2)]
                lg = SB3("lg", [7, T])
                lgT = SB3("lgT", [128, 16 * 7])
                st1 = SB3("st1", [128, 16]); st2 = SB3("st2", [128, 16])
                st3 = SB3("st3", [128, 16]); st4 = SB3("st4", [128, 16])
                pr = SB3("pr", [128, 112]); eq1 = SB3("eq1", [128, 112])
                eq2 = SB3("eq2", [128, 112]); combT = SB3("combT", [128, 112])

                p2_dve = ctx.val("dve")
                for j in range(QC):
                    bi = j % 2
                    xr_sl = XR.rearrange("p (c n) -> p c n", c=8)[:, :, j * 512:(j + 1) * 512]
                    xbf_rel[bi].acquire(*Gs)
                    W(Gs, "dve", p2_dve)
                    cv = ctx.inc("pool", g.tensor_copy(
                        xbf[bi].rearrange("p (c n) -> p c n", c=8), xr_sl))
                    xsq_rel[bi].acquire(*Ss)
                    W(Ss, "dve", p2_dve)
                    sv = ctx.inc("act", s.activation(
                        xsq[bi].rearrange("p (c n) -> p c n", c=8), xr_sl, AF.Square))
                    mvb = ln_stats(xbf[bi], xsq[bi], cv, sv, 512)
                    xbf_rel[bi].release("pe", mvb)
                    xsq_rel[bi].release("pe", mvb)
                    W(Vs, "pe", mvb)
                    lv = 0
                    for c8 in range(8):
                        ni = c8 % 2
                        nrm_rel[ni].acquire(*Vs)
                        dvt = ctx.inc("dve", v.tensor_tensor(
                            nrm[ni], XR[:, c8 * T + j * 512: c8 * T + (j + 1) * 512],
                            bank(1), ALU.mult))
                        W(Vs, "dve", dvt)
                        lv = ctx.inc("dve", v.tensor_tensor(
                            ln2[:, c8 * T + j * 512: c8 * T + (j + 1) * 512],
                            nrm[ni], bank(2), ALU.add))
                        nrm_rel[ni].release("dve", lv)
                    brel[1].release("dve", lv)
                    brel[2].release("dve", lv)
                    brel[3].acquire(*Ts)
                    W(Ts, "pool", wr_done)
                    W(Ts, "dve", lv)
                    ins = None
                    for c8 in range(8):
                        ins = t.matmul(bank(3)[0:7, :], wr_b[:, c8 * 7:(c8 + 1) * 7],
                                       ln2[:, c8 * T + j * 512: c8 * T + (j + 1) * 512],
                                       start=(c8 == 0), stop=(c8 == 7))
                    mlg = ctx.inc("pe", ins)
                    W(Ss, "pe", mlg)
                    slg = ctx.inc("act", s.activation(lg[0:7, j * 512:(j + 1) * 512],
                                                      bank(3)[0:7, :], AF.Copy))
                    brel[3].release("act", slg)
                ln2_done = ctx.val("dve")
                lg_done = ctx.val("act")
                tap("ln2", [128, 4 * T], ln2.bitcast(F32), "dve")
                tap("lg", [7, T], lg, "act")

                brel[4].acquire(*Ts)
                W(Ts, "act", lg_done)
                W(Ts, "pool", ident_done)
                for tt in range(16):
                    ins = t.transpose(PS[:, 4 * 512 + tt * 7: 4 * 512 + (tt + 1) * 7],
                                      lg[0:7, tt * 128:(tt + 1) * 128], ident[0:7, 0:7])
                mtr = ctx.inc("pe", ins)
                W(Vs, "pe", mtr)
                dtr = ctx.inc("dve", v.tensor_copy(lgT, PS[:, 4 * 512: 4 * 512 + 112]))
                brel[4].release("dve", dtr)

                v3 = lambda a: a.rearrange("p (t j) -> p t j", j=7)
                bc = lambda a: a.unsqueeze(2).broadcast_to([128, 16, 7])

                def chain(ins):
                    r = ctx.inc("dve", ins)
                    W(Vs, "dve", r)
                    return r
                W(Vs, "dve", dtr)
                chain(v.reduce_max(st1, v3(lgT), axis=AX.X))
                chain(v.tensor_tensor(v3(pr), v3(lgT), bc(st1), ALU.subtract))
                W(Ss, "dve", ctx.val("dve"))
                spr = ctx.inc("act", s.activation(pr, pr, AF.Exp))
                W(Vs, "act", spr)
                chain(v.reduce_sum(st2, v3(pr), axis=AX.X))
                chain(v.reciprocal(st2, st2))
                chain(v.tensor_tensor(v3(pr), v3(pr), bc(st2), ALU.mult))
                chain(v.reduce_max(st1, v3(pr), axis=AX.X))
                chain(v.tensor_tensor(v3(eq1), v3(pr), bc(st1), ALU.is_equal))
                chain(v.tensor_scalar(eq2, eq1, -1.0, 1.0, ALU.mult, ALU.add))
                chain(v.tensor_tensor(pr, pr, eq2, ALU.mult))
                chain(v.reduce_max(st2, v3(pr), axis=AX.X))
                chain(v.tensor_tensor(v3(eq2), v3(pr), bc(st2), ALU.is_equal))
                chain(v.tensor_tensor(st3, st1, st2, ALU.add))
                chain(v.tensor_scalar_add(st3, st3, 1e-9))
                chain(v.reciprocal(st3, st3))
                chain(v.tensor_tensor(st4, st1, st3, ALU.mult))
                chain(v.tensor_tensor(st3, st2, st3, ALU.mult))
                chain(v.tensor_tensor(v3(eq1), v3(eq1), bc(st4), ALU.mult))
                chain(v.tensor_tensor(v3(eq2), v3(eq2), bc(st3), ALU.mult))
                dcomb = ctx.inc("dve", v.tensor_tensor(combT, eq1, eq2, ALU.add))

                for b4 in range(4, 8):
                    brel[b4].acquire(*Ts)
                W(Ts, "dve", dcomb)
                for tt in range(16):
                    ins = t.transpose(PS[0:7, 2048 + tt * 128: 2048 + (tt + 1) * 128],
                                      combT[:, tt * 7:(tt + 1) * 7], ident)
                mtr2 = ctx.inc("pe", ins)
                W(Vs, "pe", mtr2)
                dcp = ctx.inc("dve", v.tensor_copy(comb, PS[0:7, 2048:4096]))
                for b4 in range(4, 8):
                    brel[b4].release("dve", dcp)
                tap("comb", [7, T], comb, "dve")
                ctx.barrier(ALL_ENG)

            with ExitStack() as fb4:
                fb4.enter_context(nc.named_scope("P4"))
                SB4 = lambda n, sh, dt=F32: fb4.enter_context(nc.sbuf_tensor(n, sh, dt)).ap()
                silg = SB4("silg", [128, T])
                silg_rel = Rel(ctx)
                hsb = SB4("hsb", [128, 8 * T], BF16)
                hsb_rel = [Rel(ctx) for _ in range(8)]
                cbs = SB4("cbs", [128, T])
                cbs_rel = Rel(ctx)
                wgb = [SB4(f"wgb{i}", [128, 8 * 128], BF16) for i in range(2)]
                wgb_rel = [Rel(ctx) for _ in range(2)]
                wub = [SB4(f"wub{i}", [128, 8 * 128], BF16) for i in range(2)]
                wub_rel = [Rel(ctx) for _ in range(2)]
                wdb = [SB4(f"wdb{i}", [128, 8 * 128], BF16) for i in range(2)]
                wdb_rel = [Rel(ctx) for _ in range(2)]

                ln2_doneP = ctx.val("dve")
                xr_last = [[ctx.val("dve")] * QC for _ in range(8)]
                for e in range(8):
                    if e > 0:
                        crow_rel.acquire(*Gs)
                        W(Gs, "dve", dcp)
                        cr = ctx.inc("qg", g.dma_start(crow, comb[e - 1:e, :]))
                        cbs_rel.acquire(*Vs)
                        for qc in range(QC):
                            bk = 2 + qc % 2
                            brel[bk].acquire(*Ts)
                            W(Ts, "qg", cr)
                            ins = t.matmul(bank(bk), ones_f,
                                           crow[0:1, qc * 512:(qc + 1) * 512],
                                           start=True, stop=True)
                            mv = ctx.inc("pe", ins)
                            W(Vs, "pe", mv)
                            dv = ctx.inc("dve", v.tensor_copy(
                                cbs[:, qc * 512:(qc + 1) * 512], bank(bk)))
                            brel[bk].release("dve", dv)
                        crow_rel.release("pe", ctx.val("pe"))
                        cb_done = ctx.val("dve")
                    else:
                        cb_done = 0

                    for db in range(4):
                        h_done = 0
                        for pc in range(8):
                            wi = pc % 2
                            wgb_rel[wi].acquire(*Gs)
                            gv = stage_cast(gate_t[e, db, pc], wgb[wi], 1024)
                            wub_rel[wi].acquire(*Gs)
                            uv = stage_cast(up_t[e, db, pc], wub[wi], 1024)
                            for qc in range(QC):
                                brel[qc].acquire(*Ts)
                            W(Ts, "pool", gv)
                            W(Ts, "dve", ln2_doneP)
                            ins = None
                            for qc in range(QC):
                                for c8 in range(8):
                                    ins = t.matmul(
                                        bank(qc), wgb[wi][:, c8 * 128:(c8 + 1) * 128],
                                        ln2[:, c8 * T + qc * 512: c8 * T + (qc + 1) * 512],
                                        start=(c8 == 0), stop=(c8 == 7))
                            mg = ctx.inc("pe", ins)
                            for qc in range(QC):
                                brel[4 + qc].acquire(*Ts)
                            W(Ts, "pool", uv)
                            for qc in range(QC):
                                for c8 in range(8):
                                    ins = t.matmul(
                                        bank(4 + qc), wub[wi][:, c8 * 128:(c8 + 1) * 128],
                                        ln2[:, c8 * T + qc * 512: c8 * T + (qc + 1) * 512],
                                        start=(c8 == 0), stop=(c8 == 7))
                            mu = ctx.inc("pe", ins)
                            wgb_rel[wi].release("pe", mg)
                            wub_rel[wi].release("pe", mu)
                            silg_rel.acquire(*Ss)
                            W(Ss, "pe", mg)
                            ssg = ctx.inc("act", s.activation(silg, PS[:, 0:2048], AF.Silu))
                            for qc in range(QC):
                                brel[qc].release("act", ssg)
                            hsb_rel[pc].acquire(*Vs)
                            W(Vs, "act", ssg)
                            W(Vs, "pe", mu)
                            if e > 0:
                                W(Vs, "dve", cb_done)
                                ctx.inc("dve", v.tensor_tensor(silg, PS[:, 2048:4096],
                                                               silg, ALU.mult))
                                W(Vs, "dve", ctx.val("dve"))
                                hv = ctx.inc("dve", v.tensor_tensor(
                                    hsb[:, pc * T:(pc + 1) * T], silg, cbs, ALU.mult))
                            else:
                                hv = ctx.inc("dve", v.tensor_tensor(
                                    hsb[:, pc * T:(pc + 1) * T], PS[:, 2048:4096],
                                    silg, ALU.mult))
                            silg_rel.release("dve", hv)
                            for qc in range(QC):
                                brel[4 + qc].release("dve", hv)
                            h_done = hv
                        for oc in range(8):
                            wi = oc % 2
                            wdb_rel[wi].acquire(*Gs)
                            dv_w = stage_cast(down_t[e, db, oc], wdb[wi], 1024)
                            for qc in range(QC):
                                bk = qc
                                brel[bk].acquire(*Ts)
                                W(Ts, "pool", dv_w)
                                W(Ts, "dve", h_done)
                                ins = None
                                for pcx in range(8):
                                    ins = t.matmul(
                                        bank(bk), wdb[wi][:, pcx * 128:(pcx + 1) * 128],
                                        hsb[:, pcx * T + qc * 512: pcx * T + (qc + 1) * 512],
                                        start=(pcx == 0), stop=(pcx == 7))
                                md = ctx.inc("pe", ins)
                                W(Vs, "pe", md)
                                W(Vs, "dve", xr_last[oc][qc])
                                dvx = ctx.inc("dve", v.tensor_tensor(
                                    XR[:, oc * T + qc * 512: oc * T + (qc + 1) * 512],
                                    XR[:, oc * T + qc * 512: oc * T + (qc + 1) * 512],
                                    bank(bk), ALU.add))
                                xr_last[oc][qc] = dvx
                                brel[bk].release("dve", dvx)
                            wdb_rel[wi].release("pe", ctx.val("pe"))
                        pe_after_down = ctx.val("pe")
                        for pc in range(8):
                            hsb_rel[pc].release("pe", pe_after_down)
                    cbs_rel.release("dve", ctx.val("dve"))
                ctx.barrier(ALL_ENG)

        W(Qs, "dve", ctx.val("dve"))
        ctx.inc("st", sync.dma_start(
            out.rearrange("(c p) n -> p c n", p=128),
            XR.rearrange("p (c n) -> p c n", c=8)))
        sync.wait_ge(ctx.sems["st"][0], ctx.val("st"))
    return nc, tap_aps



def host_inputs(inputs, core):
    b, half = core // 2, core % 2
    x = np.asarray(inputs["L0_attn_x"])
    xT = np.ascontiguousarray(x[b].T)
    x_own = np.ascontiguousarray(xT[:, half * T:(half + 1) * T])

    def tile_in_by_out(w):
        wt = np.ascontiguousarray(np.asarray(w).T)
        o = w.shape[0]
        return np.ascontiguousarray(
            wt.reshape(8, 128, o).transpose(1, 0, 2).reshape(128, 8 * o))

    wq_h = tile_in_by_out(inputs["L0_attn_Wq"])
    wk_h = tile_in_by_out(inputs["L0_attn_Wk"])
    wv_h = tile_in_by_out(inputs["L0_attn_Wv"])
    wr_h = tile_in_by_out(inputs["L0_ffn_router_w"])

    def tile_gateup(w):
        wt = np.asarray(w).T
        r = wt.reshape(8, 128, 4, 8, 128)
        return np.ascontiguousarray(r.transpose(2, 3, 1, 0, 4).reshape(4, 8, 128, 1024))

    def tile_down(w):
        wt = np.asarray(w).T
        r = wt.reshape(4, 8, 128, 8, 128)
        return np.ascontiguousarray(r.transpose(0, 3, 2, 1, 4).reshape(4, 8, 128, 1024))

    gate_h = np.stack([tile_gateup(inputs["L0_ffn_shared_gate"])]
                      + [tile_gateup(np.asarray(inputs["L0_ffn_routed_gate"])[i])
                         for i in range(7)])
    up_h = np.stack([tile_gateup(inputs["L0_ffn_shared_up"])]
                    + [tile_gateup(np.asarray(inputs["L0_ffn_routed_up"])[i])
                       for i in range(7)])
    down_h = np.stack([tile_down(inputs["L0_ffn_shared_down"])]
                      + [tile_down(np.asarray(inputs["L0_ffn_routed_down"])[i])
                         for i in range(7)])

    return dict(x_full=xT, x_own=x_own, wq=wq_h, wk=wk_h, wv=wv_h, wr=wr_h,
                gate_t=gate_h, up_t=up_h, down_t=down_h)


def kernel(**inputs):
    nc, _ = build()
    shared = None
    in_maps = []
    for c in range(8):
        m = host_inputs(inputs, c)
        if shared is None:
            shared = {k: m[k] for k in ("wq", "wk", "wv", "wr", "gate_t", "up_t", "down_t")}
        else:
            for k in shared:
                m[k] = shared[k]
        in_maps.append(m)
    res = run_bass_kernel_spmd(nc, in_maps, core_ids=list(range(8)))
    outT = np.zeros((4, D, S), dtype=np.float32)
    for c in range(8):
        b, half = c // 2, c % 2
        outT[b][:, half * T:(half + 1) * T] = res.results[c]["out"]
    return np.ascontiguousarray(outT.transpose(0, 2, 1))


# version 04
# speedup vs baseline: 1.2641x; 1.0543x over previous
import numpy as np
from contextlib import ExitStack

import concourse.bass as bass
import concourse.mybir as mybir
from concourse.bass_utils import run_bass_kernel_spmd

F32 = mybir.dt.float32
BF16 = mybir.dt.bfloat16
AF = mybir.ActivationFunctionType
ALU = mybir.AluOpType
AX = mybir.AxisListType

D = 1024
S = 4096
T = 2048
NP = 8
EPS = 1e-5
KT = S // 128
QC = T // 512
CW = 256


class Ctx:
    def __init__(self, nc, stack):
        self.nc = nc
        self.stack = stack
        self.sems = {}
        self.hwm = {}

    def sem(self, name, step=1):
        h = self.stack.enter_context(self.nc.semaphore(name))
        self.sems[name] = [h, step, 0]

    def inc(self, name, inst):
        e = self.sems[name]
        inst.then_inc(e[0], e[1])
        e[2] += e[1]
        return e[2]

    def val(self, name):
        return self.sems[name][2]

    def wait(self, ekey, eng, name, v):
        if v <= 0:
            return
        key = (ekey, name)
        if self.hwm.get(key, 0) >= v:
            return
        eng.wait_ge(self.sems[name][0], v)
        self.hwm[key] = v

    def barrier(self, engines):
        for ekey, eng in engines:
            for name in self.sems:
                self.wait(ekey, eng, name, self.val(name))


class Rel:
    def __init__(self, ctx):
        self.ctx = ctx
        self.rel = []

    def acquire(self, ekey, eng):
        for (sn, vv) in self.rel:
            self.ctx.wait(ekey, eng, sn, vv)
        self.rel = []

    def release(self, sem_name, val):
        self.rel.append((sem_name, val))


def build(taps=()):
    nc = bass.Bass("TRN2", target_bir_lowering=False)
    dr = lambda n, sh, kind: nc.dram_tensor(n, sh, F32, kind=kind).ap()

    x_full = dr("x_full", [D, S], "ExternalInput")
    x_own = dr("x_own", [D, T], "ExternalInput")
    wq = dr("wq", [128, 8 * 1024], "ExternalInput")
    wk = dr("wk", [128, 8 * 64], "ExternalInput")
    wv = dr("wv", [128, 8 * 64], "ExternalInput")
    wr = dr("wr", [128, 8 * 7], "ExternalInput")
    gate_t = dr("gate_t", [8, 4, 8, 128, 1024], "ExternalInput")
    up_t = dr("up_t", [8, 4, 8, 128, 1024], "ExternalInput")
    down_t = dr("down_t", [8, 4, 8, 128, 1024], "ExternalInput")
    out = dr("out", [D, T], "ExternalOutput")

    tap_aps = {}

    with ExitStack() as stack:
        ctx = Ctx(nc, stack)
        SB = lambda n, sh, dt=F32: stack.enter_context(nc.sbuf_tensor(n, sh, dt)).ap()
        sync, v, s, t, g = nc.sync, nc.vector, nc.scalar, nc.tensor, nc.gpsimd
        Ts, Ss, Vs, Gs, Qs = ("t", t), ("s", s), ("v", v), ("g", g), ("q", sync)
        ALL_ENG = [Ts, Ss, Vs, Gs, Qs]

        for name, step in ([("qg", 16), ("qxr", 16), ("pe", 1), ("act", 1),
                            ("dve", 1), ("pool", 1), ("st", 16)]
                           + [(f"qstg{i}", 16) for i in range(4)]
                           + [(f"qxf{i}", 16) for i in range(2)]):
            ctx.sem(name, step)
        W = lambda ee, name, val: ctx.wait(ee[0], ee[1], name, val)

        def tap(name, shape, src_ap, dep_sem):
            if name not in taps:
                return
            ap = dr("tap_" + name, shape, "ExternalOutput")
            tap_aps[name] = ap
            W(Qs, dep_sem, ctx.val(dep_sem))
            ctx.inc("st", sync.dma_start(ap, src_ap))

        XR = SB("XR", [128, 8 * T])
        PS = stack.enter_context(nc.psum_tensor("PS", [128, 4096], F32)).ap()
        bank = lambda i: PS[:, i * 512:(i + 1) * 512]
        brel = [Rel(ctx) for _ in range(8)]

        wk_b = SB("wk_b", [128, 8 * 64], BF16)
        wv_b = SB("wv_b", [128, 8 * 64], BF16)
        wr_b = SB("wr_b", [128, 8 * 7], BF16)
        ones_bf = SB("ones_bf", [128, 2], BF16)
        ones_f = SB("ones_f", [1, 128])
        eps_t = SB("eps_t", [1, 2])
        ident = SB("ident", [128, 128])
        srow = SB("srow", [1, 6 * 512])
        mrow = SB("mrow", [1, 16])
        arow = SB("arow", [1, 16])

        stg = [SB(f"stg{i}", [128, 8 * 128]) for i in range(4)]
        stg_rel = [Rel(ctx) for _ in range(4)]
        stg_i = [0]

        def stage_cast(src_ap, dst_bf_ap, n_cols):
            i = stg_i[0] % 4
            stg_i[0] += 1
            stg_rel[i].acquire(*Qs)
            dv = ctx.inc(f"qstg{i}", sync.dma_start(stg[i][:, 0:n_cols], src_ap))
            W(Gs, f"qstg{i}", dv)
            pv = ctx.inc("pool", g.tensor_copy(dst_bf_ap, stg[i][:, 0:n_cols]))
            stg_rel[i].release("pool", pv)
            return pv

        ctx.inc("dve", v.memset(ones_bf, 1.0))
        ctx.inc("dve", v.memset(ones_f, 1.0))
        consts_dve = ctx.inc("dve", v.memset(eps_t, EPS))
        ctx.inc("pool", g.memset(ident, 1.0))
        W(Gs, "pool", ctx.val("pool"))
        ident_done = ctx.inc("pool", g.affine_select(
            ident, ident, [[1, 128]], ALU.is_equal, 0.0, base=0, channel_multiplier=-1))

        xr_load = ctx.inc("qxr", sync.dma_start(
            XR.rearrange("p (c n) -> p c n", c=8),
            x_own.rearrange("(c p) n -> p c n", p=128)))

        wk_done = stage_cast(wk, wk_b, 8 * 64)
        wv_done = stage_cast(wv, wv_b, 8 * 64)
        wr_done = stage_cast(wr, wr_b, 8 * 7)

        def ln_stats(xbf_ap, xsq_ap, ready_pool, ready_act, cw):
            brel[0].acquire(*Ts)
            W(Ts, "dve", consts_dve)
            W(Ts, "pool", ready_pool)
            W(Ts, "act", ready_act)
            ins = None
            for c8 in range(8):
                ins = t.matmul(bank(0)[0:1, 0:cw], ones_bf[:, 0:1],
                               xbf_ap[:, c8 * cw:(c8 + 1) * cw],
                               start=(c8 == 0), stop=(c8 == 7))
            mv_sum = ctx.inc("pe", ins)
            brel[1].acquire(*Ts)
            for c8 in range(8):
                ins = t.matmul(bank(1)[0:1, 0:cw], ones_bf[:, 0:1],
                               xsq_ap[:, c8 * cw:(c8 + 1) * cw],
                               start=(c8 == 0), stop=(c8 == 7))
            mv_sq = ctx.inc("pe", ins)
            W(Vs, "pe", mv_sq)
            dv1 = ctx.inc("dve", v.tensor_scalar_mul(srow[0:1, 0:cw], bank(0)[0:1, 0:cw],
                                                     1.0 / D))
            dv2 = ctx.inc("dve", v.tensor_scalar_mul(srow[0:1, 512:512 + cw],
                                                     bank(1)[0:1, 0:cw], 1.0 / D))
            brel[0].release("dve", dv1)
            W(Vs, "dve", dv2)
            ctx.inc("dve", v.tensor_mul(srow[0:1, 1024:1024 + cw], srow[0:1, 0:cw],
                                        srow[0:1, 0:cw]))
            W(Vs, "dve", ctx.val("dve"))
            dvv = ctx.inc("dve", v.tensor_sub(srow[0:1, 1024:1024 + cw],
                                              srow[0:1, 512:512 + cw],
                                              srow[0:1, 1024:1024 + cw]))
            brel[1].release("dve", dvv)
            W(Ss, "dve", dvv)
            W(Ss, "dve", consts_dve)
            ctx.inc("act", s.activation(srow[0:1, 1536:1536 + cw], srow[0:1, 1024:1024 + cw],
                                        AF.Ln, bias=eps_t[0:1, 0:1], scale=1.0))
            W(Ss, "act", ctx.val("act"))
            sv2 = ctx.inc("act", s.activation(srow[0:1, 1536:1536 + cw],
                                              srow[0:1, 1536:1536 + cw], AF.Exp, scale=-0.5))
            W(Vs, "act", sv2)
            dvn = ctx.inc("dve", v.scalar_tensor_tensor(
                srow[0:1, 2048:2048 + cw], srow[0:1, 0:cw], -1.0, srow[0:1, 1536:1536 + cw],
                ALU.mult, ALU.mult))
            brel[1].acquire(*Ts)
            brel[2].acquire(*Ts)
            W(Ts, "act", sv2)
            W(Ts, "dve", dvn)
            t.matmul(bank(1)[:, 0:cw], ones_f, srow[0:1, 1536:1536 + cw],
                     start=True, stop=True)
            ins = t.matmul(bank(2)[:, 0:cw], ones_f, srow[0:1, 2048:2048 + cw],
                           start=True, stop=True)
            return ctx.inc("pe", ins)

        with ExitStack() as fa:
            SBA = lambda n, sh, dt=F32: fa.enter_context(nc.sbuf_tensor(n, sh, dt)).ap()
            Q = SBA("Q", [128, 8 * T], BF16)
            K2 = SBA("K2", [128, S], BF16)
            VA = SBA("VA", [128, KT * 65], BF16)
            va_ones = ctx.inc("dve", v.memset(VA, 1.0))

            with ExitStack() as fa1:
                fa1.enter_context(nc.named_scope("P1"))
                SB1 = lambda n, sh, dt=F32: fa1.enter_context(nc.sbuf_tensor(n, sh, dt)).ap()
                wq_b = SB1("wq_b", [128, 8 * 1024], BF16)
                xf = [SB1(f"xf{i}", [128, 8 * CW]) for i in range(2)]
                xf_rel = [Rel(ctx) for _ in range(2)]
                xbf = [SB1(f"xbf{i}", [128, 8 * CW], BF16) for i in range(2)]
                xbf_rel = [Rel(ctx) for _ in range(2)]
                xsq = [SB1(f"xsq{i}", [128, 8 * CW], BF16) for i in range(2)]
                xsq_rel = [Rel(ctx) for _ in range(2)]
                lnb = [SB1(f"lnb{i}", [128, 8 * CW], BF16) for i in range(2)]
                lnb_rel = [Rel(ctx) for _ in range(2)]
                nrm = [SB1(f"nrm{i}", [128, CW]) for i in range(2)]
                nrm_rel = [Rel(ctx) for _ in range(2)]

                wq_done = 0
                for c8 in range(8):
                    wq_done = stage_cast(wq[:, c8 * 1024:(c8 + 1) * 1024],
                                         wq_b[:, c8 * 1024:(c8 + 1) * 1024], 1024)

                def ln_chunk(src_dram, bi, ln_out):
                    xf_rel[bi].acquire(*Qs)
                    dv = ctx.inc(f"qxf{bi}", sync.dma_start(
                        xf[bi].rearrange("p (c n) -> p c n", c=8), src_dram))
                    W(Gs, f"qxf{bi}", dv)
                    xbf_rel[bi].acquire(*Gs)
                    cv = ctx.inc("pool", g.tensor_copy(xbf[bi], xf[bi]))
                    W(Ss, f"qxf{bi}", dv)
                    xsq_rel[bi].acquire(*Ss)
                    sv = ctx.inc("act", s.activation(xsq[bi], xf[bi], AF.Square))
                    mvb = ln_stats(xbf[bi], xsq[bi], cv, sv, CW)
                    xbf_rel[bi].release("pe", mvb)
                    xsq_rel[bi].release("pe", mvb)
                    W(Vs, "pe", mvb)
                    lv = 0
                    for c8 in range(8):
                        ni = c8 % 2
                        nrm_rel[ni].acquire(*Vs)
                        dvt = ctx.inc("dve", v.tensor_tensor(
                            nrm[ni], xf[bi][:, c8 * CW:(c8 + 1) * CW],
                            bank(1)[:, 0:CW], ALU.mult))
                        W(Vs, "dve", dvt)
                        lv = ctx.inc("dve", v.tensor_tensor(
                            ln_out[:, c8 * CW:(c8 + 1) * CW], nrm[ni],
                            bank(2)[:, 0:CW], ALU.add))
                        nrm_rel[ni].release("dve", lv)
                    xf_rel[bi].release("dve", lv)
                    brel[1].release("dve", lv)
                    brel[2].release("dve", lv)
                    return lv

                xr_src = x_full.rearrange("(c p) n -> p c n", p=128)
                for j in range(S // CW):
                    bi = j % 2
                    lnb_rel[bi].acquire(*Vs)
                    lv = ln_chunk(xr_src[:, :, j * CW:(j + 1) * CW], bi, lnb[bi])
                    brel[3].acquire(*Ts)
                    W(Ts, "pool", wk_done)
                    W(Ts, "dve", lv)
                    ins = None
                    for c8 in range(8):
                        ins = t.matmul(bank(3)[0:64, 0:CW], wk_b[:, c8 * 64:(c8 + 1) * 64],
                                       lnb[bi][:, c8 * CW:(c8 + 1) * CW],
                                       start=(c8 == 0), stop=(c8 == 7))
                    mk = ctx.inc("pe", ins)
                    W(Ss, "pe", mk)
                    s.activation(K2[0:64, j * CW:(j + 1) * CW], bank(3)[0:64, 0:CW], AF.Copy)
                    se = ctx.inc("act", s.activation(K2[64:128, j * CW:(j + 1) * CW],
                                                     bank(3)[0:64, 0:CW], AF.Copy))
                    brel[3].release("act", se)
                    brel[4].acquire(*Ts)
                    W(Ts, "pool", wv_done)
                    ins = None
                    for kt4 in range(CW // 128):
                        for c8 in range(8):
                            ins = t.matmul(
                                PS[:, 4 * 512 + kt4 * 64: 4 * 512 + (kt4 + 1) * 64],
                                lnb[bi][:, c8 * CW + kt4 * 128: c8 * CW + (kt4 + 1) * 128],
                                wv_b[:, c8 * 64:(c8 + 1) * 64],
                                start=(c8 == 0), stop=(c8 == 7))
                    mvv = ctx.inc("pe", ins)
                    W(Vs, "pe", mvv)
                    W(Vs, "dve", va_ones)
                    kt0 = j * (CW // 128)
                    dvv2 = ctx.inc("dve", v.tensor_copy(
                        VA[:, kt0 * 65:(kt0 + CW // 128) * 65]
                        .rearrange("p (k n) -> p k n", k=CW // 128)[:, :, 0:64],
                        PS[:, 4 * 512: 4 * 512 + (CW // 128) * 64]
                        .rearrange("p (k n) -> p k n", k=CW // 128)))
                    brel[4].release("dve", dvv2)
                    lnb_rel[bi].release("pe", mvv)

                xo_src = x_own.rearrange("(c p) n -> p c n", p=128)
                for j in range(T // CW):
                    bi = j % 2
                    lnb_rel[bi].acquire(*Vs)
                    lv = ln_chunk(xo_src[:, :, j * CW:(j + 1) * CW], bi, lnb[bi])
                    for hp in range(8):
                        bk = 5 + (hp % 2)
                        brel[bk].acquire(*Ts)
                        W(Ts, "pool", wq_done)
                        W(Ts, "dve", lv)
                        ins = None
                        for c8 in range(8):
                            ins = t.matmul(
                                bank(bk)[:, 0:CW],
                                wq_b[:, c8 * 1024 + hp * 128: c8 * 1024 + (hp + 1) * 128],
                                lnb[bi][:, c8 * CW:(c8 + 1) * CW],
                                start=(c8 == 0), stop=(c8 == 7))
                        mq = ctx.inc("pe", ins)
                        W(Ss, "pe", mq)
                        qe = ctx.inc("act", s.activation(
                            Q[:, hp * T + j * CW: hp * T + (j + 1) * CW],
                            bank(bk)[:, 0:CW], AF.Copy))
                        brel[bk].release("act", qe)
                    lnb_rel[bi].release("pe", ctx.val("pe"))

                tap("lnq", [128, 8 * CW // 2], lnb[(T // CW - 1) % 2].bitcast(F32), "dve")
                ctx.barrier(ALL_ENG)

            p1_act = ctx.val("act")
            p1_dve = ctx.val("dve")
            tap("K2", [128, S // 2], K2.bitcast(F32), "act")
            tap("VA", [128, (KT * 65 + 1) // 2], VA.bitcast(F32), "dve")
            tap("Q", [128, 4 * T], Q.bitcast(F32), "act")

            with ExitStack() as fa2:
                fa2.enter_context(nc.named_scope("P2"))
                SB2 = lambda n, sh, dt=F32: fa2.enter_context(nc.sbuf_tensor(n, sh, dt)).ap()
                ksq = SB2("ksq", [64, S], BF16)
                qsq = SB2("qsq", [128, T], BF16)
                qsq_rel = Rel(ctx)
                PT = [SB2(f"PT{i}", [128, 1024], BF16) for i in range(2)]
                PT_rel = [Rel(ctx) for _ in range(2)]
                att_s = SB2("att_s", [128, 2 * 512])
                bias_hd = SB2("bias_hd", [128, 2])
                rdrow = SB2("rdrow", [1, 2 * 512])

                W(Ss, "act", p1_act)
                kq = ctx.inc("act", s.activation(ksq, K2[0:64, :], AF.Square))
                for c in range(8):
                    brel[7].acquire(*Ts)
                    W(Ts, "act", kq)
                    ins = t.matmul(bank(7)[0:1, :], ones_bf[0:64, 0:1],
                                   ksq[:, c * 512:(c + 1) * 512], start=True, stop=True)
                    mv = ctx.inc("pe", ins)
                    W(Vs, "pe", mv)
                    dv = ctx.inc("dve", v.reduce_max(mrow[0:1, c:c + 1], bank(7)[0:1, :],
                                                     axis=AX.X))
                    brel[7].release("dve", dv)
                W(Vs, "dve", ctx.val("dve"))
                ctx.inc("dve", v.reduce_max(mrow[0:1, 8:9], mrow[0:1, 0:8], axis=AX.X))

                for hp in range(NP):
                    qsq_rel.acquire(*Ss)
                    W(Ss, "act", p1_act)
                    sv = ctx.inc("act", s.activation(qsq, Q[:, hp * T:(hp + 1) * T],
                                                     AF.Square))
                    for qc in range(QC):
                        brel[7].acquire(*Ts)
                        W(Ts, "act", sv)
                        ins = t.matmul(bank(7)[0:1, :], ones_bf[:, 0:1],
                                       qsq[:, qc * 512:(qc + 1) * 512],
                                       start=True, stop=True)
                        mv = ctx.inc("pe", ins)
                        W(Vs, "pe", mv)
                        dv = ctx.inc("dve", v.reduce_max(
                            mrow[0:1, 12 + qc:13 + qc], bank(7)[0:1, :], axis=AX.X))
                        brel[7].release("dve", dv)
                    W(Vs, "dve", ctx.val("dve"))
                    ctx.inc("dve", v.reduce_max(mrow[0:1, 10:11], mrow[0:1, 12:16],
                                                axis=AX.X))
                    W(Vs, "dve", ctx.val("dve"))
                    ctx.inc("dve", v.tensor_add(arow[0:1, 0:1], mrow[0:1, 10:11],
                                                mrow[0:1, 8:9]))
                    W(Vs, "dve", ctx.val("dve"))
                    dvb2 = ctx.inc("dve", v.tensor_scalar_mul(
                        arow[0:1, 0:1], arow[0:1, 0:1], -1.0 / 16.0))
                    brel[7].acquire(*Ts)
                    W(Ts, "dve", dvb2)
                    ins = t.matmul(PS[:, 7 * 512:7 * 512 + 1], ones_f,
                                   arow[0:1, 0:1], start=True, stop=True)
                    mv = ctx.inc("pe", ins)
                    W(Vs, "pe", mv)
                    W(Vs, "act", ctx.val("act"))
                    dv = ctx.inc("dve", v.tensor_copy(bias_hd[:, 0:1],
                                                      PS[:, 7 * 512:7 * 512 + 1]))
                    brel[7].release("dve", dv)
                    qsq_rel.release("pe", ctx.val("pe"))
                    bias_done = ctx.val("dve")

                    for qc in range(QC):
                        qcol = slice(hp * T + qc * 512, hp * T + (qc + 1) * 512)
                        brel[4].acquire(*Ts)
                        brel[5].acquire(*Ts)
                        W(Ts, "dve", p1_dve)
                        W(Ts, "act", p1_act)
                        pv = 0
                        exp_of = {}

                        def issue_scores(kt):
                            sb = 2 * (kt % 2)
                            brel[sb].acquire(*Ts)
                            brel[sb + 1].acquire(*Ts)
                            t.matmul(bank(sb), K2[0:64, kt * 128:(kt + 1) * 128],
                                     Q[0:64, qcol], start=True, stop=True,
                                     tile_position=(0, 0))
                            ins = t.matmul(bank(sb + 1), K2[64:128, kt * 128:(kt + 1) * 128],
                                           Q[64:128, qcol], start=True, stop=True,
                                           tile_position=(64, 0))
                            mv = ctx.inc("pe", ins)
                            pt = kt % 2
                            PT_rel[pt].acquire(*Ss)
                            W(Ss, "pe", mv)
                            W(Ss, "dve", bias_done)
                            ev = ctx.inc("act", s.activation(
                                PT[pt], PS[:, sb * 512: sb * 512 + 1024], AF.Exp,
                                bias=bias_hd[:, 0:1], scale=0.125))
                            brel[sb].release("act", ev)
                            brel[sb + 1].release("act", ev)
                            exp_of[kt] = ev

                        issue_scores(0)
                        for kt in range(KT):
                            if kt + 1 < KT:
                                issue_scores(kt + 1)
                            pt = kt % 2
                            W(Ts, "act", exp_of[kt])
                            t.matmul(bank(4)[0:65, :], VA[:, kt * 65:(kt + 1) * 65],
                                     PT[pt][:, 0:512], start=(kt == 0), stop=(kt == KT - 1))
                            pv = ctx.inc("pe", t.matmul(
                                bank(5)[0:65, :], VA[:, kt * 65:(kt + 1) * 65],
                                PT[pt][:, 512:1024], start=(kt == 0), stop=(kt == KT - 1)))
                            PT_rel[pt].release("pe", pv)
                        for hh in range(2):
                            ab = 4 + hh
                            rows = slice(64 * hh, 64 * hh + 64)
                            W(Vs, "pe", pv)
                            dr_ = ctx.inc("dve", v.reciprocal(
                                rdrow[0:1, hh * 512:(hh + 1) * 512],
                                PS[64:65, ab * 512:(ab + 1) * 512]))
                            brel[6].acquire(*Ts)
                            W(Ts, "dve", dr_)
                            ins = t.matmul(PS[rows, 6 * 512:7 * 512], ones_f[0:1, 0:64],
                                           rdrow[0:1, hh * 512:(hh + 1) * 512],
                                           start=True, stop=True)
                            mv = ctx.inc("pe", ins)
                            W(Ss, "pe", mv)
                            sc = ctx.inc("act", s.activation(
                                att_s[rows, hh * 512:(hh + 1) * 512],
                                bank(ab)[0:64, :], AF.Copy))
                            brel[ab].release("act", sc)
                            W(Vs, "act", sc)
                            W(Vs, "pe", mv)
                            dv1 = ctx.inc("dve", v.tensor_tensor(
                                att_s[rows, hh * 512:(hh + 1) * 512],
                                att_s[rows, hh * 512:(hh + 1) * 512],
                                PS[rows, 6 * 512:7 * 512], ALU.mult))
                            brel[6].release("dve", dv1)
                            W(Vs, "dve", dv1)
                            W(Vs, "qxr", xr_load)
                            ctx.inc("dve", v.tensor_tensor(
                                XR[rows, hp * T + qc * 512: hp * T + (qc + 1) * 512],
                                XR[rows, hp * T + qc * 512: hp * T + (qc + 1) * 512],
                                att_s[rows, hh * 512:(hh + 1) * 512], ALU.add))
                ctx.barrier(ALL_ENG)

            tap("X2", [128, 8 * T], XR, "dve")
            ctx.barrier(ALL_ENG)

        with ExitStack() as fb:
            SBB = lambda n, sh, dt=F32: fb.enter_context(nc.sbuf_tensor(n, sh, dt)).ap()
            ln2 = SBB("ln2", [128, 8 * T], BF16)
            comb = SBB("comb", [7, T])
            crow = SBB("crow", [1, T])
            crow_rel = Rel(ctx)
            dcp = 0

            with ExitStack() as fb3:
                fb3.enter_context(nc.named_scope("P3"))
                SB3 = lambda n, sh, dt=F32: fb3.enter_context(nc.sbuf_tensor(n, sh, dt)).ap()
                xbf = [SB3(f"xbf3_{i}", [128, 8 * 512], BF16) for i in range(2)]
                xbf_rel = [Rel(ctx) for _ in range(2)]
                xsq = [SB3(f"xsq3_{i}", [128, 8 * 512], BF16) for i in range(2)]
                xsq_rel = [Rel(ctx) for _ in range(2)]
                nrm = [SB3(f"nrm3_{i}", [128, 512]) for i in range(2)]
                nrm_rel = [Rel(ctx) for _ in range(2)]
                lg = SB3("lg", [7, T])
                lgT = SB3("lgT", [128, 16 * 7])
                st1 = SB3("st1", [128, 16]); st2 = SB3("st2", [128, 16])
                st3 = SB3("st3", [128, 16]); st4 = SB3("st4", [128, 16])
                pr = SB3("pr", [128, 112]); eq1 = SB3("eq1", [128, 112])
                eq2 = SB3("eq2", [128, 112]); combT = SB3("combT", [128, 112])

                p2_dve = ctx.val("dve")
                for j in range(QC):
                    bi = j % 2
                    xr_sl = XR.rearrange("p (c n) -> p c n", c=8)[:, :, j * 512:(j + 1) * 512]
                    xbf_rel[bi].acquire(*Gs)
                    W(Gs, "dve", p2_dve)
                    cv = ctx.inc("pool", g.tensor_copy(
                        xbf[bi].rearrange("p (c n) -> p c n", c=8), xr_sl))
                    xsq_rel[bi].acquire(*Ss)
                    W(Ss, "dve", p2_dve)
                    sv = ctx.inc("act", s.activation(
                        xsq[bi].rearrange("p (c n) -> p c n", c=8), xr_sl, AF.Square))
                    mvb = ln_stats(xbf[bi], xsq[bi], cv, sv, 512)
                    xbf_rel[bi].release("pe", mvb)
                    xsq_rel[bi].release("pe", mvb)
                    W(Vs, "pe", mvb)
                    lv = 0
                    for c8 in range(8):
                        ni = c8 % 2
                        nrm_rel[ni].acquire(*Vs)
                        dvt = ctx.inc("dve", v.tensor_tensor(
                            nrm[ni], XR[:, c8 * T + j * 512: c8 * T + (j + 1) * 512],
                            bank(1), ALU.mult))
                        W(Vs, "dve", dvt)
                        lv = ctx.inc("dve", v.tensor_tensor(
                            ln2[:, c8 * T + j * 512: c8 * T + (j + 1) * 512],
                            nrm[ni], bank(2), ALU.add))
                        nrm_rel[ni].release("dve", lv)
                    brel[1].release("dve", lv)
                    brel[2].release("dve", lv)
                    brel[3].acquire(*Ts)
                    W(Ts, "pool", wr_done)
                    W(Ts, "dve", lv)
                    ins = None
                    for c8 in range(8):
                        ins = t.matmul(bank(3)[0:7, :], wr_b[:, c8 * 7:(c8 + 1) * 7],
                                       ln2[:, c8 * T + j * 512: c8 * T + (j + 1) * 512],
                                       start=(c8 == 0), stop=(c8 == 7))
                    mlg = ctx.inc("pe", ins)
                    W(Ss, "pe", mlg)
                    slg = ctx.inc("act", s.activation(lg[0:7, j * 512:(j + 1) * 512],
                                                      bank(3)[0:7, :], AF.Copy))
                    brel[3].release("act", slg)
                ln2_done = ctx.val("dve")
                lg_done = ctx.val("act")
                tap("ln2", [128, 4 * T], ln2.bitcast(F32), "dve")
                tap("lg", [7, T], lg, "act")

                brel[4].acquire(*Ts)
                W(Ts, "act", lg_done)
                W(Ts, "pool", ident_done)
                for tt in range(16):
                    ins = t.transpose(PS[:, 4 * 512 + tt * 7: 4 * 512 + (tt + 1) * 7],
                                      lg[0:7, tt * 128:(tt + 1) * 128], ident[0:7, 0:7])
                mtr = ctx.inc("pe", ins)
                W(Vs, "pe", mtr)
                dtr = ctx.inc("dve", v.tensor_copy(lgT, PS[:, 4 * 512: 4 * 512 + 112]))
                brel[4].release("dve", dtr)

                v3 = lambda a: a.rearrange("p (t j) -> p t j", j=7)
                bc = lambda a: a.unsqueeze(2).broadcast_to([128, 16, 7])

                def chain(ins):
                    r = ctx.inc("dve", ins)
                    W(Vs, "dve", r)
                    return r
                W(Vs, "dve", dtr)
                chain(v.reduce_max(st1, v3(lgT), axis=AX.X))
                chain(v.tensor_tensor(v3(pr), v3(lgT), bc(st1), ALU.subtract))
                W(Ss, "dve", ctx.val("dve"))
                spr = ctx.inc("act", s.activation(pr, pr, AF.Exp))
                W(Vs, "act", spr)
                chain(v.reduce_sum(st2, v3(pr), axis=AX.X))
                chain(v.reciprocal(st2, st2))
                chain(v.tensor_tensor(v3(pr), v3(pr), bc(st2), ALU.mult))
                chain(v.reduce_max(st1, v3(pr), axis=AX.X))
                chain(v.tensor_tensor(v3(eq1), v3(pr), bc(st1), ALU.is_equal))
                chain(v.tensor_scalar(eq2, eq1, -1.0, 1.0, ALU.mult, ALU.add))
                chain(v.tensor_tensor(pr, pr, eq2, ALU.mult))
                chain(v.reduce_max(st2, v3(pr), axis=AX.X))
                chain(v.tensor_tensor(v3(eq2), v3(pr), bc(st2), ALU.is_equal))
                chain(v.tensor_tensor(st3, st1, st2, ALU.add))
                chain(v.tensor_scalar_add(st3, st3, 1e-9))
                chain(v.reciprocal(st3, st3))
                chain(v.tensor_tensor(st4, st1, st3, ALU.mult))
                chain(v.tensor_tensor(st3, st2, st3, ALU.mult))
                chain(v.tensor_tensor(v3(eq1), v3(eq1), bc(st4), ALU.mult))
                chain(v.tensor_tensor(v3(eq2), v3(eq2), bc(st3), ALU.mult))
                dcomb = ctx.inc("dve", v.tensor_tensor(combT, eq1, eq2, ALU.add))

                for b4 in range(4, 8):
                    brel[b4].acquire(*Ts)
                W(Ts, "dve", dcomb)
                for tt in range(16):
                    ins = t.transpose(PS[0:7, 2048 + tt * 128: 2048 + (tt + 1) * 128],
                                      combT[:, tt * 7:(tt + 1) * 7], ident)
                mtr2 = ctx.inc("pe", ins)
                W(Vs, "pe", mtr2)
                dcp = ctx.inc("dve", v.tensor_copy(comb, PS[0:7, 2048:4096]))
                for b4 in range(4, 8):
                    brel[b4].release("dve", dcp)
                tap("comb", [7, T], comb, "dve")
                ctx.barrier(ALL_ENG)

            with ExitStack() as fb4:
                fb4.enter_context(nc.named_scope("P4"))
                SB4 = lambda n, sh, dt=F32: fb4.enter_context(nc.sbuf_tensor(n, sh, dt)).ap()
                silg = SB4("silg", [128, T])
                silg_rel = Rel(ctx)
                hsb = SB4("hsb", [128, 8 * T], BF16)
                hsb_rel = [Rel(ctx) for _ in range(8)]
                cbs = SB4("cbs", [128, T])
                cbs_rel = Rel(ctx)
                wgb = [SB4(f"wgb{i}", [128, 8 * 128], BF16) for i in range(2)]
                wgb_rel = [Rel(ctx) for _ in range(2)]
                wub = [SB4(f"wub{i}", [128, 8 * 128], BF16) for i in range(2)]
                wub_rel = [Rel(ctx) for _ in range(2)]
                wdb = [SB4(f"wdb{i}", [128, 8 * 128], BF16) for i in range(2)]
                wdb_rel = [Rel(ctx) for _ in range(2)]

                ln2_doneP = ctx.val("dve")
                xr_last = [[ctx.val("dve")] * QC for _ in range(8)]
                for e in range(8):
                    if e > 0:
                        crow_rel.acquire(*Gs)
                        W(Gs, "dve", dcp)
                        cr = ctx.inc("qg", g.dma_start(crow, comb[e - 1:e, :]))
                        cbs_rel.acquire(*Vs)
                        for qc in range(QC):
                            bk = 2 + qc % 2
                            brel[bk].acquire(*Ts)
                            W(Ts, "qg", cr)
                            ins = t.matmul(bank(bk), ones_f,
                                           crow[0:1, qc * 512:(qc + 1) * 512],
                                           start=True, stop=True)
                            mv = ctx.inc("pe", ins)
                            W(Vs, "pe", mv)
                            dv = ctx.inc("dve", v.tensor_copy(
                                cbs[:, qc * 512:(qc + 1) * 512], bank(bk)))
                            brel[bk].release("dve", dv)
                        crow_rel.release("pe", ctx.val("pe"))
                        cb_done = ctx.val("dve")
                    else:
                        cb_done = 0

                    for db in range(4):
                        h_done = 0
                        for pc in range(8):
                            wi = pc % 2
                            wgb_rel[wi].acquire(*Gs)
                            gv = stage_cast(gate_t[e, db, pc], wgb[wi], 1024)
                            wub_rel[wi].acquire(*Gs)
                            uv = stage_cast(up_t[e, db, pc], wub[wi], 1024)
                            for qc in range(QC):
                                brel[qc].acquire(*Ts)
                            W(Ts, "pool", gv)
                            W(Ts, "dve", ln2_doneP)
                            ins = None
                            for qc in range(QC):
                                for c8 in range(8):
                                    ins = t.matmul(
                                        bank(qc), wgb[wi][:, c8 * 128:(c8 + 1) * 128],
                                        ln2[:, c8 * T + qc * 512: c8 * T + (qc + 1) * 512],
                                        start=(c8 == 0), stop=(c8 == 7))
                            mg = ctx.inc("pe", ins)
                            for qc in range(QC):
                                brel[4 + qc].acquire(*Ts)
                            W(Ts, "pool", uv)
                            for qc in range(QC):
                                for c8 in range(8):
                                    ins = t.matmul(
                                        bank(4 + qc), wub[wi][:, c8 * 128:(c8 + 1) * 128],
                                        ln2[:, c8 * T + qc * 512: c8 * T + (qc + 1) * 512],
                                        start=(c8 == 0), stop=(c8 == 7))
                            mu = ctx.inc("pe", ins)
                            wgb_rel[wi].release("pe", mg)
                            wub_rel[wi].release("pe", mu)
                            silg_rel.acquire(*Ss)
                            W(Ss, "pe", mg)
                            ssg = ctx.inc("act", s.activation(silg, PS[:, 0:2048], AF.Silu))
                            for qc in range(QC):
                                brel[qc].release("act", ssg)
                            hsb_rel[pc].acquire(*Vs)
                            W(Vs, "act", ssg)
                            W(Vs, "pe", mu)
                            if e > 0:
                                W(Vs, "dve", cb_done)
                                ctx.inc("dve", v.tensor_tensor(silg, PS[:, 2048:4096],
                                                               silg, ALU.mult))
                                W(Vs, "dve", ctx.val("dve"))
                                hv = ctx.inc("dve", v.tensor_tensor(
                                    hsb[:, pc * T:(pc + 1) * T], silg, cbs, ALU.mult))
                            else:
                                hv = ctx.inc("dve", v.tensor_tensor(
                                    hsb[:, pc * T:(pc + 1) * T], PS[:, 2048:4096],
                                    silg, ALU.mult))
                            silg_rel.release("dve", hv)
                            for qc in range(QC):
                                brel[4 + qc].release("dve", hv)
                            h_done = hv
                        for oc in range(8):
                            wi = oc % 2
                            wdb_rel[wi].acquire(*Gs)
                            dv_w = stage_cast(down_t[e, db, oc], wdb[wi], 1024)
                            for qc in range(QC):
                                bk = qc
                                brel[bk].acquire(*Ts)
                                W(Ts, "pool", dv_w)
                                W(Ts, "dve", h_done)
                                ins = None
                                for pcx in range(8):
                                    ins = t.matmul(
                                        bank(bk), wdb[wi][:, pcx * 128:(pcx + 1) * 128],
                                        hsb[:, pcx * T + qc * 512: pcx * T + (qc + 1) * 512],
                                        start=(pcx == 0), stop=(pcx == 7))
                                md = ctx.inc("pe", ins)
                                W(Vs, "pe", md)
                                W(Vs, "dve", xr_last[oc][qc])
                                dvx = ctx.inc("dve", v.tensor_tensor(
                                    XR[:, oc * T + qc * 512: oc * T + (qc + 1) * 512],
                                    XR[:, oc * T + qc * 512: oc * T + (qc + 1) * 512],
                                    bank(bk), ALU.add))
                                xr_last[oc][qc] = dvx
                                brel[bk].release("dve", dvx)
                            wdb_rel[wi].release("pe", ctx.val("pe"))
                        pe_after_down = ctx.val("pe")
                        for pc in range(8):
                            hsb_rel[pc].release("pe", pe_after_down)
                    cbs_rel.release("dve", ctx.val("dve"))
                ctx.barrier(ALL_ENG)

        W(Qs, "dve", ctx.val("dve"))
        ctx.inc("st", sync.dma_start(
            out.rearrange("(c p) n -> p c n", p=128),
            XR.rearrange("p (c n) -> p c n", c=8)))
        sync.wait_ge(ctx.sems["st"][0], ctx.val("st"))
    return nc, tap_aps



def host_inputs(inputs, core):
    b, half = core // 2, core % 2
    x = np.asarray(inputs["L0_attn_x"])
    xT = np.ascontiguousarray(x[b].T)
    x_own = np.ascontiguousarray(xT[:, half * T:(half + 1) * T])

    def tile_in_by_out(w):
        wt = np.ascontiguousarray(np.asarray(w).T)
        o = w.shape[0]
        return np.ascontiguousarray(
            wt.reshape(8, 128, o).transpose(1, 0, 2).reshape(128, 8 * o))

    wq_h = tile_in_by_out(inputs["L0_attn_Wq"])
    wk_h = tile_in_by_out(inputs["L0_attn_Wk"])
    wv_h = tile_in_by_out(inputs["L0_attn_Wv"])
    wr_h = tile_in_by_out(inputs["L0_ffn_router_w"])

    def tile_gateup(w):
        wt = np.asarray(w).T
        r = wt.reshape(8, 128, 4, 8, 128)
        return np.ascontiguousarray(r.transpose(2, 3, 1, 0, 4).reshape(4, 8, 128, 1024))

    def tile_down(w):
        wt = np.asarray(w).T
        r = wt.reshape(4, 8, 128, 8, 128)
        return np.ascontiguousarray(r.transpose(0, 3, 2, 1, 4).reshape(4, 8, 128, 1024))

    gate_h = np.stack([tile_gateup(inputs["L0_ffn_shared_gate"])]
                      + [tile_gateup(np.asarray(inputs["L0_ffn_routed_gate"])[i])
                         for i in range(7)])
    up_h = np.stack([tile_gateup(inputs["L0_ffn_shared_up"])]
                    + [tile_gateup(np.asarray(inputs["L0_ffn_routed_up"])[i])
                       for i in range(7)])
    down_h = np.stack([tile_down(inputs["L0_ffn_shared_down"])]
                      + [tile_down(np.asarray(inputs["L0_ffn_routed_down"])[i])
                         for i in range(7)])

    return dict(x_full=xT, x_own=x_own, wq=wq_h, wk=wk_h, wv=wv_h, wr=wr_h,
                gate_t=gate_h, up_t=up_h, down_t=down_h)


def kernel(**inputs):
    nc, _ = build()
    shared = None
    in_maps = []
    for c in range(8):
        m = host_inputs(inputs, c)
        if shared is None:
            shared = {k: m[k] for k in ("wq", "wk", "wv", "wr", "gate_t", "up_t", "down_t")}
        else:
            for k in shared:
                m[k] = shared[k]
        in_maps.append(m)
    res = run_bass_kernel_spmd(nc, in_maps, core_ids=list(range(8)))
    outT = np.zeros((4, D, S), dtype=np.float32)
    for c in range(8):
        b, half = c // 2, c % 2
        outT[b][:, half * T:(half + 1) * T] = res.results[c]["out"]
    return np.ascontiguousarray(outT.transpose(0, 2, 1))
